# version 09
# baseline (speedup 1.0000x reference)
import math
from contextlib import ExitStack

import numpy as np
import ml_dtypes
import concourse.bass as bass
import concourse.mybir as mybir
from concourse.ap import AP
from concourse.bass_utils import run_bass_kernel_spmd

F32 = mybir.dt.float32
BF16 = mybir.dt.bfloat16
AF = mybir.ActivationFunctionType
ALU = mybir.AluOpType
AX = mybir.AxisListType

D = 1024
S = 2048
NT = 16
NH = 16
HP = 8
KT = 16
DC = 8
DFF = 4096
FC = 32
NE = 9
G2 = 4
EB = 4096

ENGINES = ("sync", "tensor", "scalar", "vector", "gpsimd")
FLUSH_PER_PHASE = True
PE, ACT, DVE, SYNC, POOL = "tensor", "scalar", "vector", "sync", "gpsimd"


class Sem:
    __slots__ = ("h", "n")

    def __init__(self, h):
        self.h = h
        self.n = 0


class Builder:
    def __init__(self):
        self.nc = bass.Bass("TRN2", target_bir_lowering=False)
        self.stack = ExitStack()
        self.ops = {k: [] for k in ENGINES}
        self.scopes = {}
        self.block = None
        self.final_waits = []

    def E(self, eng, fn):
        self.ops[eng].append(fn)

    def wait(self, eng, sem, val):
        if val:
            self.E(eng, lambda e, h=sem.h, v=val: e.wait_ge(h, v))

    def do(self, eng, meth, *a, inc=None, lp=False, plain=False, **kw):
        incs = []
        if inc is not None:
            incs = inc if isinstance(inc, list) else [inc]
        for s, amt in incs:
            s.n += amt
        caps = [(s.h, amt) for s, amt in incs]
        if lp:
            nc = self.nc

            def fn(e, meth=meth, a=a, kw=kw, caps=caps):
                with nc.allow_low_precision(reason="intentional bf16 output"):
                    getattr(e, meth)(*a, **kw)
                ins = e.drain()
                for h, amt in caps:
                    ins = ins.then_inc(h, amt)
            self.E(eng, fn)
            return incs[0][0].n if incs else None

        if eng in ("vector", "scalar") and caps and not plain:
            def fn(e, meth=meth, a=a, kw=kw, caps=caps):
                getattr(e, meth)(*a, **kw)
                ins = e.drain()
                for h, amt in caps:
                    ins = ins.then_inc(h, amt)
        else:
            def fn(e, meth=meth, a=a, kw=kw, caps=caps):
                ins = getattr(e, meth)(*a, **kw)
                for h, amt in caps:
                    ins = ins.then_inc(h, amt)
        self.E(eng, fn)
        return incs[0][0].n if incs else None

    def bar(self, eng):
        self.E(eng, lambda e: e.drain())

    def sem(self, name):
        h = self.stack.enter_context(self.nc.semaphore(name))
        return Sem(h)

    def flush(self):
        ops = self.ops
        self.ops = {k: [] for k in ENGINES}
        for name in ENGINES:
            if ops[name]:
                getattr(self.block, name)(lambda e, o=ops[name]: [f(e) for f in o])

    def alloc(self, scope, kind, name, shape, dtype, side="left"):
        if scope not in self.scopes:
            self.scopes[scope] = ExitStack()
        if kind == "s":
            return self.scopes[scope].enter_context(
                self.nc.sbuf_tensor(name, shape, dtype, side=side))
        return self.scopes[scope].enter_context(self.nc.psum_tensor(name, shape, dtype))

    def close_scope(self, *names):
        if FLUSH_PER_PHASE:
            self.flush()
        for n in names:
            if n in self.scopes:
                self.scopes.pop(n).close()


def build_program(stage="full", serial_attn=False):
    b = Builder()
    nc = b.nc

    def din(name, shape, dtype=F32):
        return nc.dram_tensor(name, shape, dtype, kind="ExternalInput").ap()

    x_in = din("x", [S, D])
    cvec = din("cvec", [1, D])
    wmod_ch = din("wmod_ch", [12, D, 512])
    qkvT_ch = din("qkvT_ch", [24, D, 128])
    owT = din("owT", [D, D])
    Tp = din("Tp", [NH, 128, EB])
    rwT_ch = din("rwT_ch", [8, 128, 8])
    rb = din("rb", [8, 1])
    w1_ch = din("w1_ch", [NE, FC, D, 128])
    w2T = din("w2T", [NE, DFF, D])
    b1t = din("b1t", [NE, 128, FC])
    B2in = din("B2in", [NE, D])
    identf = din("identf", [128, 128], F32)

    out = nc.dram_tensor("out", [S, D], F32, kind="ExternalOutput").ap()
    x1d = nc.dram_tensor("x1d", [S, D], F32, kind="Internal").ap()
    aTd = nc.dram_tensor("aTd", [NH, 65, S], BF16, kind="Internal").ap()
    dbg = None
    if stage != "full":
        dbg = nc.dram_tensor("dbg", [128, DC * S], F32, kind="ExternalOutput").ap()

    sm = {n: b.sem(n) for n in [
        "xld", "wm", "qw", "kw", "vw", "ow", "rw", "w1", "w2", "bload",
        "Tld", "Eld", "Est", "x1st", "x1ld", "outst", "dbgst", "aTst", "aTld",
        "pe", "act", "dve",
    ]}
    slot_sems = {}

    def ssem(name, nslots):
        slot_sems[name] = [b.sem(f"{name}_s{i}") for i in range(nslots)]
        return slot_sems[name]

    ss_wm = ssem("wm2", 2)
    ss_qw = ssem("qw2", 2)
    ss_kw = ssem("kw2", 2)
    ss_tb = ssem("tb2", 2)
    ss_w1 = ssem("w1r", 4)
    ss_w2 = ssem("w2r", 2)
    ss_x = ssem("xr", 2)
    ss_x4 = ssem("x4r", 2)
    ss_x6 = ssem("x6r", 2)
    ss_ar = ssem("arr", 4)
    ss_sp = ssem("spr", 2)
    ss_x1o = ssem("x1o", 2)
    ss_oo = ssem("oor", 2)
    b.final_waits = [ss_oo[0], ss_oo[1]]
    b.block = b.stack.enter_context(nc.Block())

    A = b.alloc
    ident_b = A("static", "s", "ident_b", [128, 128], BF16)
    ident_f = A("static", "s", "ident_f", [128, 128], F32)
    onesf = A("static", "s", "onesf", [1, 64], BF16)
    zer128 = A("static", "s", "zer128", [128, 128], F32)
    neg16 = A("static", "s", "neg16", [128, 1], F32)
    comb_sb = A("static", "s", "comb_sb", [128, NT * NE], F32)
    combT_sb = A("static", "s", "combT_sb", [16, S], BF16)
    logits_sb = A("static", "s", "logits_sb", [128, NT * 8], F32)
    gb_sb = A("static", "s", "gb_sb", [128, D], F32)
    rb_sb = A("static", "s", "rb_sb", [8, 1], F32)
    h2T_sb = A("h2T", "s", "h2T_sb", [128, DC * S], BF16)
    mga = A("modv", "s", "mga", [128, D], F32)
    mcb = A("modv", "s", "mcb", [128, D], F32)
    msb = A("modv", "s", "msb", [128, D], F32)
    hT_sb = A("hT", "s", "hT_sb", [128, DC * S], BF16, side="right")
    Vhat_sb = A("vhat", "s", "Vhat_sb", [128, NT * 1040], BF16, side="right")
    Vw_sb = A("p2", "s", "Vw_sb", [128, DC * 1024], BF16, side="right")
    msa = A("modv2", "s", "msa", [128, D], F32, side="right")
    mca = A("modv2", "s", "mca", [128, D], F32, side="right")

    hT_view = hT_sb[:, :].rearrange("p (dc s) -> p dc s", s=S)
    h2T_view = h2T_sb[:, :].rearrange("p (dc s) -> p dc s", s=S)
    mod_t = {"sa": msa, "ca": mca, "ga": mga, "sb": msb, "cb": mcb, "gb": gb_sb}
    mod_order = ["sa", "ca", "ga", "sb", "cb", "gb"]

    b.do(POOL, "dma_start", out=ident_b[:, :], in_=identf[:, :], inc=(sm["Tld"], 16))
    b.do(SYNC, "dma_start", out=ident_f[:, :], in_=identf[:, :], inc=(sm["Tld"], 16))
    b.do(SYNC, "dma_start", out=rb_sb[:, :], in_=rb[:, :], inc=(sm["Tld"], 16))
    t_consts = sm["Tld"].n
    b.wait(DVE, sm["Tld"], t_consts)
    b.do(DVE, "memset", onesf[:, :], 1.0)
    b.do(DVE, "memset", neg16[:, :], -16.0)
    b.do(DVE, "memset", zer128[:, :], 0.0, inc=(sm["dve"], 1))
    t_zer = sm["dve"].n

    ct = A("p0", "s", "ct", [128, 8], F32, side="right")
    sct = A("p0", "s", "sct", [128, 8], F32, side="right")
    srep = A("p0", "s", "srep", [128, 8 * 128], BF16, side="right")
    wm_sb = [A("p0", "s", f"wm_sb{i}", [128, 8 * 512], BF16, side="right") for i in range(2)]
    scr1k = A("p0", "s", "scr1k", [128, D], F32, side="right")
    scr1b = A("p0", "s", "scr1b", [128, D], F32, side="right")
    psMod = [A("p0p", "p", "psMod0", [128, 512], F32)] * 2

    for dc in range(DC):
        b.do(SYNC, "dma_start", out=ct[:, dc:dc + 1],
             in_=AP(tensor=cvec.tensor, offset=dc * 128, ap=[[1, 128], [1, 1]]),
             inc=(sm["xld"], 16))
    t_ct = sm["xld"].n
    b.wait(ACT, sm["xld"], t_ct)
    b.do(ACT, "activation", out=sct[:, :], in_=ct[:, :], func=AF.Silu)
    b.bar(ACT)
    b.wait(ACT, sm["dve"], t_zer)
    for dc in range(DC):
        b.do(ACT, "activation", out=srep[:, dc * 128:(dc + 1) * 128], in_=zer128[:, :],
             func=AF.Identity, bias=sct[:, dc:dc + 1],
             inc=(sm["act"], 1) if dc == DC - 1 else None)
    t_srep = sm["act"].n
    b.wait(PE, sm["act"], t_srep)

    wm_free = [0, 0]
    ps_free = [0, 0]
    for n in range(12):
        par = n % 2
        vec = mod_order[n // 2]
        col = (n % 2) * 512
        b.wait(POOL, sm["pe"], wm_free[par])
        b.do(POOL, "dma_start",
             out=wm_sb[par][:, :].rearrange("p (dc j) -> p dc j", j=512),
             in_=wmod_ch[n].rearrange("(dc p) j -> p dc j", p=128),
             inc=(ss_wm[par], 16))
        b.wait(PE, ss_wm[par], ss_wm[par].n)
        b.wait(PE, sm["dve"], max(ps_free))
        for dc in range(DC):
            b.do(PE, "matmul", psMod[par][:, :], srep[:, dc * 128:(dc + 1) * 128],
                 wm_sb[par][:, dc * 512:(dc + 1) * 512],
                 start=(dc == 0), stop=(dc == DC - 1),
                 inc=(sm["pe"], 1) if dc == DC - 1 else None)
        wm_free[par] = sm["pe"].n
        b.wait(DVE, sm["pe"], sm["pe"].n)
        b.do(DVE, "tensor_copy", mod_t[vec][:, col:col + 512], psMod[par][:, :], inc=(sm["dve"], 1))
        ps_free[par] = sm["dve"].n
    b.do(DVE, "tensor_scalar", mca[:, :], mca[:, :], 1.0, None, ALU.add)
    b.do(DVE, "tensor_scalar", mcb[:, :], mcb[:, :], 1.0, None, ALU.add)
    for g in (mga, gb_sb):
        b.do(DVE, "tensor_scalar", scr1k[:, :], g[:, :], 0.0, None, ALU.is_ge)
        b.do(DVE, "tensor_scalar", scr1k[:, :], scr1k[:, :], 2.0, -1.0, ALU.mult, ALU.add)
        b.do(DVE, "tensor_scalar", scr1b[:, :], g[:, :], -1.0, None, ALU.mult)
        b.do(DVE, "tensor_max", g[:, :], g[:, :], scr1b[:, :])
        b.do(DVE, "tensor_scalar", g[:, :], g[:, :], 0.25, None, ALU.max)
        b.do(DVE, "tensor_mul", g[:, :], g[:, :], scr1k[:, :])
    b.do(DVE, "tensor_copy", scr1k[:, 0:1], scr1k[:, 0:1], inc=(sm["dve"], 1))
    t_mod = sm["dve"].n
    for dcd in range(DC):
        for j in range(8):
            b.do(POOL, "dma_start",
                 out=Vw_sb[:, dcd * 1024 + j * 128: dcd * 1024 + (j + 1) * 128],
                 in_=qkvT_ch[16 + j][dcd * 128:(dcd + 1) * 128, :],
                 inc=(sm["vw"], 16))
    t_vw = sm["vw"].n

    xb = [A("p1", "s", f"xb{i}", [128, D], F32, side="right") for i in range(2)]
    hbuf = [A("p1", "s", f"hbuf{i}", [128, D], BF16, side="right") for i in range(2)]
    stat = A("p1", "s", "stat", [128, 8], F32, side="right")
    psSq = A("p1p", "p", "psSq", [128, D], F32)
    psT = A("p1p", "p", "psT", [128, D], BF16)
    psV = [A("p1p", "p", f"psV{i}", [128, D], F32) for i in range(2)]

    def emit_ln(src_tile, cav, sav, hout, scr, st8, psq, xc=None):
        b.do(DVE, "reduce_sum", st8[:, 0:1], src_tile, axis=AX.X)
        b.bar(DVE)
        b.do(DVE, "tensor_scalar", st8[:, 1:2], st8[:, 0:1], -1.0 / D, None, ALU.mult)
        b.bar(DVE)
        dst = src_tile if xc is None else xc
        b.do(DVE, "tensor_scalar", dst, src_tile, st8[:, 1:2], None, ALU.add, inc=(sm["dve"], 1))
        t1 = sm["dve"].n
        b.wait(ACT, sm["dve"], t1)
        b.do(ACT, "activation", out=psq[:, :], in_=dst, func=AF.Square,
             accum_out=st8[:, 2:3], inc=(sm["act"], 1))
        t2 = sm["act"].n
        b.wait(DVE, sm["act"], t2)
        b.do(DVE, "tensor_scalar", st8[:, 3:4], st8[:, 2:3], 1.0 / D, 1e-5, ALU.mult, ALU.add)
        b.bar(DVE)
        b.do(DVE, "reciprocal", st8[:, 4:5], st8[:, 3:4], inc=(sm["dve"], 1))
        t3 = sm["dve"].n
        b.wait(ACT, sm["dve"], t3)
        b.do(ACT, "activation", out=st8[:, 5:6], in_=st8[:, 4:5], func=AF.Sqrt, inc=(sm["act"], 1))
        t4 = sm["act"].n
        b.do(DVE, "tensor_mul", scr, dst, cav[:, :])
        b.wait(DVE, sm["act"], t4)
        b.do(DVE, "scalar_tensor_tensor", hout, scr, st8[:, 5:6],
             sav[:, :], ALU.mult, ALU.add, inc=(sm["dve"], 1))
        return sm["dve"].n

    vones = Vhat_sb[:, :].rearrange("p (s hd) -> p s hd", hd=1040)[:, :, :].rearrange(
        "p s (h d) -> p s h d", d=65)[:, :, :, 64:65]
    b.do(DVE, "memset", vones, 1.0, inc=(sm["dve"], 1))
    t_vones = sm["dve"].n
    b.wait(DVE, sm["dve"], t_mod)
    b.wait(PE, sm["vw"], t_vw)
    x_free = [0, 0]
    h_free = [0, 0]
    psv_free = [0, 0]
    tp_ev = 0
    for sc in range(NT):
        par = sc % 2
        b.wait(SYNC, sm["dve"], x_free[par])
        b.do(SYNC, "dma_start", out=xb[par][:, :], in_=x_in[sc * 128:(sc + 1) * 128, :],
             inc=(ss_x[par], 16))
        b.wait(DVE, ss_x[par], ss_x[par].n)
        b.wait(DVE, sm["pe"], h_free[par])
        t_h = emit_ln(xb[par][:, :], mca, msa, hbuf[par][:, :], scr1k[:, :], stat, psSq)
        x_free[par] = t_h
        b.wait(PE, sm["dve"], t_h)
        b.wait(PE, sm["dve"], tp_ev)
        for dc in range(DC):
            b.do(PE, "transpose", psT[:, dc * 128:(dc + 1) * 128], hbuf[par][:, dc * 128:(dc + 1) * 128],
                 ident_b[:, :], inc=(sm["pe"], 1) if dc == DC - 1 else None)
        h_free[par] = sm["pe"].n
        b.wait(DVE, sm["pe"], sm["pe"].n)
        b.do(DVE, "tensor_copy", hT_view[:, :, sc * 128:(sc + 1) * 128],
             psT[:, :].rearrange("p (dc j) -> p dc j", j=128), inc=(sm["dve"], 1))
        tp_ev = sm["dve"].n
        b.wait(PE, sm["dve"], tp_ev)
        b.wait(PE, sm["dve"], psv_free[par])
        for dcd in range(DC):
            for nn in range(2):
                b.do(PE, "matmul", psV[par][:, nn * 512:(nn + 1) * 512],
                     hT_view[:, dcd, sc * 128:(sc + 1) * 128],
                     Vw_sb[:, dcd * 1024 + nn * 512: dcd * 1024 + (nn + 1) * 512],
                     start=(dcd == 0), stop=(dcd == DC - 1),
                     inc=(sm["pe"], 1) if (dcd == DC - 1 and nn == 1) else None)
        b.wait(DVE, sm["pe"], sm["pe"].n)
        if sc == 0:
            b.wait(DVE, sm["dve"], t_vones)
        vview = Vhat_sb[:, sc * 1040:(sc + 1) * 1040].rearrange("p (h d) -> p h d", d=65)[:, :, 0:64]
        b.do(DVE, "tensor_copy", vview, psV[par][:, :].rearrange("p (h d) -> p h d", d=64),
             inc=(sm["dve"], 1))
        psv_free[par] = sm["dve"].n
    t_hT = sm["dve"].n
    t_vhat = sm["dve"].n
    t_vpe = sm["pe"].n
    b.close_scope("p1p", "p0p", "p1", "p0", "modv2", "p2")

    if stage == "hT":
        b.wait(POOL, sm["dve"], t_hT)
        b.do(POOL, "dma_start", out=dbg[:, :], in_=hT_sb[:, :], inc=(sm["dbgst"], 16))
        return _finish(b, sm)

    qw_sb = [A("attn", "s", f"qw_sb{i}", [128, DC * 128], BF16, side="right") for i in range(2)]
    kw_sb = [A("attn", "s", f"kw_sb{i}", [128, DC * 128], BF16, side="right") for i in range(2)]
    qT_sb = [A("attn", "s", f"qT_sb{i}", [128, S], BF16, side="right") for i in range(2)]
    kT_sb = [A("attn", "s", f"kT_sb{i}", [128, S], BF16, side="right") for i in range(2)]
    Tb = [A("attn", "s", f"Tb{i}", [128, EB], BF16, side="right") for i in range(2)]
    ptb = [A("attn", "s", f"ptb{i}", [128, S], BF16, side="right") for i in range(4)]
    aT_pair = [A("attnL", "s", f"aT_pair{i}", [128, S], BF16) for i in range(2)]
    psQK = A("attnp", "p", "psQK", [128, S], F32)
    psaT = A("attnp", "p", "psaT", [128, S], F32)

    b.wait(SYNC, sm["dve"], t_vhat)
    b.wait(POOL, sm["dve"], t_vhat)

    Tb_free = [0, 0]
    Tb_tok = {}

    def load_Tb(h):
        par = h % 2
        b.wait(POOL, sm["pe"], Tb_free[par])
        b.do(POOL, "dma_start", out=Tb[par][:, :], in_=Tp[h], inc=(ss_tb[par], 16))
        Tb_tok[h] = (ss_tb[par], ss_tb[par].n)

    qkw_free = [0, 0]
    qkw_tok = {}

    def load_qk_w(hp):
        par = hp % 2
        b.wait(POOL, sm["pe"], qkw_free[par])
        for dcd in range(DC):
            b.do(POOL, "dma_start", out=qw_sb[par][:, dcd * 128:(dcd + 1) * 128],
                 in_=qkvT_ch[hp][dcd * 128:(dcd + 1) * 128, :], inc=(ss_qw[par], 16))
            b.do(POOL, "dma_start", out=kw_sb[par][:, dcd * 128:(dcd + 1) * 128],
                 in_=qkvT_ch[8 + hp][dcd * 128:(dcd + 1) * 128, :], inc=(ss_kw[par], 16))
        qkw_tok[hp] = (ss_qw[par].n, ss_kw[par].n)

    load_qk_w(0)
    load_qk_w(1)
    load_Tb(0)
    load_Tb(1)

    qkT_free = [0, 0]
    psQK_free = 0
    psaT_free = 0
    aTp_free = [0, 0]
    ptb_free = [0] * 4

    for hp in range(HP):
        par = hp % 2
        b.wait(PE, ss_qw[hp % 2], qkw_tok[hp][0])
        b.wait(PE, ss_kw[hp % 2], qkw_tok[hp][1])
        b.wait(PE, sm["pe"], qkT_free[par])
        b.wait(PE, sm["act"], psQK_free)
        if hp == 0:
            b.wait(PE, sm["dve"], t_vhat)
        for wi, (wsb, dst, scale) in enumerate(
                ((qw_sb[par], qT_sb[par], 1.0), (kw_sb[par], kT_sb[par], 0.125))):
            if wi == 1:
                b.wait(PE, sm["act"], sm["act"].n)
            for dcd in range(DC):
                for nn in range(4):
                    b.do(PE, "matmul", psQK[:, nn * 512:(nn + 1) * 512],
                         wsb[:, dcd * 128:(dcd + 1) * 128],
                         hT_view[:, dcd, nn * 512:(nn + 1) * 512],
                         start=(dcd == 0), stop=(dcd == DC - 1),
                         inc=(sm["pe"], 1) if (dcd == DC - 1 and nn == 3) else None)
            t_qk = sm["pe"].n
            b.wait(ACT, sm["pe"], t_qk)
            b.do(ACT, "activation", out=dst[:, :], in_=psQK[:, :], func=AF.Copy, scale=scale,
                 inc=(sm["act"], 1))
        qkw_free[par] = sm["pe"].n
        t_qkevac = sm["act"].n
        psQK_free = t_qkevac
        if hp + 1 < HP:
            load_qk_w(hp + 1)

        psQK_free_lo = psQK_free
        for lh in range(2):
            h = 2 * hp + lh
            base = 64 * lh
            b.wait(PE, sm["dve"], psaT_free)
            b.wait(PE, Tb_tok[h][0], Tb_tok[h][1])
            exp_u = {}
            pv_tok = {}

            def emit_unit(kt, qq):
                sl = (4 * kt + qq) % 4
                pslot = psQK[:, sl * 512:(sl + 1) * 512]
                if kt == 0:
                    b.wait(PE, sm["act"], psQK_free_lo if qq < 2 else psQK_free)
                if kt >= 1:
                    b.wait(PE, sm["act"], exp_u[(kt - 1, qq)])
                off = 2048 - 128 * kt + qq * 512
                b.do(PE, "matmul", pslot, ident_b[:, :], Tb[h % 2][:, off:off + 512],
                     start=True, stop=False)
                b.do(PE, "matmul", pslot,
                     kT_sb[par][base:base + 64, kt * 128:(kt + 1) * 128],
                     qT_sb[par][base:base + 64, qq * 512:(qq + 1) * 512],
                     start=False, stop=True, inc=(sm["pe"], 1))
                t_sc = sm["pe"].n
                drained = (kt == KT - 1 and qq >= 2)
                b.wait(ACT, sm["pe"], t_sc)
                if qq == 0:
                    b.wait(ACT, sm["pe"], ptb_free[kt % 4])
                b.do(ACT, "activation", out=ptb[kt % 4][:, qq * 512:(qq + 1) * 512],
                     in_=pslot, func=AF.Exp, bias=neg16[:, 0:1], inc=(sm["act"], 1),
                     plain=not drained)
                exp_u[(kt, qq)] = sm["act"].n

            def emit_pv(kt):
                lagtok = exp_u[(kt, 3)] + (0 if kt == KT - 1 else 4)
                b.wait(PE, sm["act"], lagtok)
                for nn in range(4):
                    b.do(PE, "matmul", psaT[0:65, nn * 512:(nn + 1) * 512],
                         Vhat_sb[:, kt * 1040 + 65 * h: kt * 1040 + 65 * h + 65],
                         ptb[kt % 4][:, nn * 512:(nn + 1) * 512],
                         start=(kt == 0), stop=(kt == KT - 1),
                         inc=(sm["pe"], 1) if nn == 3 else None)
                pv_tok[kt] = sm["pe"].n
                ptb_free[kt % 4] = sm["pe"].n

            for kt in range(KT):
                for qq in range(4):
                    emit_unit(kt, qq)
                if kt >= 1:
                    emit_pv(kt - 1)
            emit_pv(KT - 1)
            t_pv = sm["pe"].n
            psQK_free_lo = exp_u[(KT - 1, 3)]
            psQK_free = exp_u[(KT - 1, 3)]
            parh = h % 2
            b.wait(DVE, sm["pe"], t_pv)
            b.wait(DVE, ss_sp[parh], aTp_free[parh])
            b.do(DVE, "tensor_copy", aT_pair[parh][0:65, :], psaT[0:65, :], inc=(sm["dve"], 1),
                 lp=True)
            psaT_free = sm["dve"].n
            b.wait(SYNC, sm["dve"], psaT_free)
            b.do(SYNC, "dma_start", out=aTd[h], in_=aT_pair[parh][0:65, :],
                 inc=(ss_sp[parh], 16))
            aTp_free[parh] = ss_sp[parh].n
            if h + 2 < NH:
                Tb_free[h % 2] = t_pv
                load_Tb(h + 2)
        qkT_free[par] = sm["pe"].n

    t_attn = sm["dve"].n
    b.close_scope("attnp", "attnL", "attn", "vhat", "hT")


    aT4_sb = A("p4", "s", "aT4_sb", [128, DC * S], BF16, side="right")
    araw = [A("p4", "s", f"araw{i}", [65, S], BF16, side="right") for i in range(4)]
    ones2 = A("p4", "s", "ones2", [128, 64], BF16, side="right")
    recB = [A("p4", "s", f"recB{i}", [64, S], BF16, side="right") for i in range(2)]
    psN = [A("p4pre", "p", f"psN{i}", [64, S], F32) for i in range(2)]
    ow_sb = A("p4", "s", "ow_sb", [128, DC * 1024], BF16, side="right")
    x4b = [A("p4", "s", f"x4b{i}", [128, D], F32, side="right") for i in range(2)]
    x1b = [A("p4", "s", f"x1b{i}", [128, D], F32, side="right") for i in range(2)]
    h4buf = [A("p4", "s", f"h4buf{i}", [128, D], BF16, side="right") for i in range(2)]
    scr4 = A("p4", "s", "scr4", [128, D], F32, side="right")
    scr4b = A("p4", "s", "scr4b", [128, D], F32, side="right")
    stat4 = A("p4", "s", "stat4", [128, 8], F32, side="right")

    b.wait(POOL, sm["dve"], t_attn)
    b.wait(SYNC, sm["dve"], t_attn)
    for dcd in range(DC):
        b.do(POOL, "dma_start", out=ow_sb[:, dcd * 1024:(dcd + 1) * 1024],
             in_=owT[dcd * 128:(dcd + 1) * 128, :], inc=(sm["ow"], 16))
    t_ow = sm["ow"].n
    b.do(DVE, "memset", ones2[:, :], 1.0, inc=(sm["dve"], 1))
    t_ones2 = sm["dve"].n
    araw_free = [0, 0, 0, 0]
    psN_free = [0, 0]
    t_norm = 0
    for h in range(NH):
        parh = h % 4
        b.wait(SYNC, sm["dve"], araw_free[parh])
        b.do(SYNC, "dma_start", out=araw[parh][0:65, :], in_=aTd[h], inc=(ss_ar[parh], 16))
        t_al = ss_ar[parh].n
        b.wait(PE, ss_ar[parh], t_al)
        b.wait(PE, sm["dve"], psN_free[h % 2])
        if h == 0:
            b.wait(PE, sm["dve"], t_ones2)
        for nn in range(4):
            b.do(PE, "matmul", psN[h % 2][0:64, nn * 512:(nn + 1) * 512], ones2[64:65, 0:64],
                 araw[parh][64:65, nn * 512:(nn + 1) * 512], start=True, stop=True,
                 inc=(sm["pe"], 1) if nn == 3 else None)
        t_rb = sm["pe"].n
        b.wait(DVE, sm["pe"], t_rb)
        b.do(DVE, "reciprocal", recB[h % 2][:, :], psN[h % 2][0:64, :], lp=True)
        b.do(DVE, "tensor_mul",
             aT4_sb[64 * (h % 2):64 * (h % 2) + 64, (h // 2) * 2048:(h // 2 + 1) * 2048],
             araw[parh][0:64, :], recB[h % 2][:, :], inc=(sm["dve"], 1))
        t_norm = sm["dve"].n
        psN_free[h % 2] = t_norm
        araw_free[parh] = t_norm
    b.close_scope("p4pre")
    psY = [A("p4p", "p", f"psY{i}", [128, D], F32) for i in range(2)]
    psT4 = A("p4p", "p", "psT4", [128, D], BF16)
    psSq4 = A("p4p", "p", "psSq4", [128, D], F32)
    b.wait(PE, sm["ow"], t_ow)
    b.wait(PE, sm["dve"], t_norm)

    psy_free = [0, 0]
    x4_free = [0, 0]
    x1_free = [0, 0]
    h4_free = [0, 0]
    tp4_ev = 0
    for sc in range(NT):
        par = sc % 2
        b.wait(PE, sm["dve"], psy_free[par])
        for hc in range(DC):
            for nn in range(2):
                b.do(PE, "matmul", psY[par][:, nn * 512:(nn + 1) * 512],
                     aT4_sb[:, hc * 2048 + sc * 128: hc * 2048 + (sc + 1) * 128],
                     ow_sb[:, hc * 1024 + nn * 512: hc * 1024 + (nn + 1) * 512],
                     start=(hc == 0), stop=(hc == DC - 1),
                     inc=(sm["pe"], 1) if (hc == DC - 1 and nn == 1) else None)
        t_oy = sm["pe"].n
        b.wait(SYNC, sm["dve"], x4_free[par])
        b.do(SYNC, "dma_start", out=x4b[par][:, :], in_=x_in[sc * 128:(sc + 1) * 128, :],
             inc=(ss_x4[par], 16))
        t_x = ss_x4[par].n
        b.wait(DVE, sm["pe"], t_oy)
        b.do(DVE, "tensor_mul", scr4[:, :], psY[par][:, :], mga[:, :], inc=(sm["dve"], 1))
        psy_free[par] = sm["dve"].n
        b.wait(DVE, ss_x4[par], t_x)
        b.wait(DVE, ss_x1o[par], x1_free[par])
        b.do(DVE, "tensor_add", x1b[par][:, :], x4b[par][:, :], scr4[:, :], inc=(sm["dve"], 1))
        t_x1add = sm["dve"].n
        x4_free[par] = t_x1add
        b.wait(SYNC, sm["dve"], t_x1add)
        b.do(SYNC, "dma_start", out=x1d[sc * 128:(sc + 1) * 128, :], in_=x1b[par][:, :],
             inc=(ss_x1o[par], 16))
        x1_free[par] = ss_x1o[par].n
        b.wait(DVE, sm["pe"], h4_free[par])
        t_h2 = emit_ln(x1b[par][:, :], mcb, msb, h4buf[par][:, :], scr4[:, :], stat4, psSq4,
                       xc=scr4b[:, :])
        b.wait(PE, sm["dve"], t_h2)
        b.wait(PE, sm["dve"], tp4_ev)
        for dcd in range(DC):
            b.do(PE, "transpose", psT4[:, dcd * 128:(dcd + 1) * 128],
                 h4buf[par][:, dcd * 128:(dcd + 1) * 128],
                 ident_b[:, :], inc=(sm["pe"], 1) if dcd == DC - 1 else None)
        h4_free[par] = sm["pe"].n
        b.wait(DVE, sm["pe"], sm["pe"].n)
        b.do(DVE, "tensor_copy", h2T_view[:, :, sc * 128:(sc + 1) * 128],
             psT4[:, :].rearrange("p (dc j) -> p dc j", j=128), inc=(sm["dve"], 1))
        tp4_ev = sm["dve"].n
    t_h2T = sm["dve"].n
    t_x1_all = (ss_x1o[0].n, ss_x1o[1].n)
    b.close_scope("p4p", "p4", "modv")

    if stage == "h2T":
        b.wait(POOL, sm["dve"], t_h2T)
        b.do(POOL, "dma_start", out=dbg[:, :], in_=h2T_sb[:, :], inc=(sm["dbgst"], 16))
        return _finish(b, sm)

    rw_sb = A("p5", "s", "rw_sb", [128, 64], BF16)
    lt_sbf = A("p5", "s", "lt_sbf", [128, S], F32)
    ext = A("p5", "s", "ext", [128, 8], F32)
    m8t = A("p5", "s", "m8t", [128, 8], F32)
    mvt = A("p5", "s", "mvt", [128, 8], F32)
    mrt = A("p5", "s", "mrt", [128, 8], F32)
    tin = A("p5", "s", "tin", [128, 8], F32)
    negm5 = A("p5", "s", "negm5", [128, 1], F32)
    cb128 = A("p5", "s", "cb128", [128, 128], BF16)
    psL = A("p5p", "p", "psL", [128, S], F32)
    psT5 = A("p5p", "p", "psT5", [128, 128], F32)
    psT5b = A("p5p", "p", "psT5b", [128, 128], BF16)

    b.wait(POOL, sm["dve"], t_h2T)
    b.do(POOL, "dma_start", out=rw_sb[:, :].rearrange("p (dc e) -> p dc e", e=8),
         in_=rwT_ch[:, :, :].transpose([1, 0, 2]), inc=(sm["rw"], 16))
    b.do(DVE, "memset", lt_sbf[:, :], 0.0)
    b.do(DVE, "memset", cb128[:, :], 0.0)
    combview = comb_sb[:, :].rearrange("p (sc e) -> p sc e", e=NE)[:, :, 8:9]
    b.do(DVE, "memset", combview, 1.0, inc=(sm["dve"], 1))
    t_msets = sm["dve"].n
    b.wait(PE, sm["rw"], sm["rw"].n)
    b.wait(PE, sm["dve"], t_h2T)
    b.wait(PE, sm["dve"], t_msets)
    for dcd in range(DC):
        for nn in range(4):
            b.do(PE, "matmul", psL[0:8, nn * 512:(nn + 1) * 512],
                 rw_sb[:, dcd * 8:(dcd + 1) * 8],
                 h2T_view[:, dcd, nn * 512:(nn + 1) * 512],
                 start=(dcd == 0), stop=(dcd == DC - 1),
                 inc=(sm["pe"], 1) if (dcd == DC - 1 and nn == 3) else None)
    t_l = sm["pe"].n
    b.wait(ACT, sm["pe"], t_l)
    b.wait(ACT, sm["Tld"], t_consts)
    b.do(ACT, "activation", out=lt_sbf[0:8, :], in_=psL[0:8, :], func=AF.Identity,
         bias=rb_sb[0:8, 0:1], inc=(sm["act"], 1))
    t_le = sm["act"].n
    b.wait(PE, sm["act"], t_le)
    tp5_ev = 0
    for sc in range(NT):
        b.wait(PE, sm["dve"], tp5_ev)
        b.do(PE, "transpose", psT5[:, :], lt_sbf[:, sc * 128:(sc + 1) * 128], ident_f[:, :],
             inc=(sm["pe"], 1))
        b.wait(DVE, sm["pe"], sm["pe"].n)
        b.do(DVE, "tensor_copy", logits_sb[:, sc * 8:(sc + 1) * 8], psT5[:, 0:8], inc=(sm["dve"], 1))
        tp5_ev = sm["dve"].n

    ext_free = 0
    tp5b_done = 0
    for sc in range(NT):
        lsl = logits_sb[:, sc * 8:(sc + 1) * 8]
        b.do(DVE, "reduce_max", negm5[:, 0:1], lsl, axis=AX.X, negate=True, inc=(sm["dve"], 1))
        t_nm = sm["dve"].n
        b.wait(ACT, sm["dve"], t_nm)
        b.wait(ACT, sm["dve"], ext_free)
        b.do(ACT, "activation", out=ext[:, :], in_=lsl, func=AF.Exp, bias=negm5[:, 0:1],
             inc=(sm["act"], 1))
        t_ex = sm["act"].n
        b.wait(DVE, sm["act"], t_ex)
        b.do(DVE, "max", m8t[:, :], ext[:, :])
        b.bar(DVE)
        b.do(DVE, "tensor_copy", mvt[:, 0:6], m8t[:, 2:8])
        b.do(DVE, "tensor_copy", mvt[:, 6:7], m8t[:, 7:8])
        b.do(DVE, "tensor_copy", mvt[:, 7:8], m8t[:, 7:8])
        b.bar(DVE)
        b.do(DVE, "match_replace", mrt[:, :], mvt[:, :], ext[:, :], 0.0)
        b.do(DVE, "tensor_scalar", tin[:, 0:1], m8t[:, 0:1], m8t[:, 1:2], None, ALU.add)
        b.bar(DVE)
        b.do(DVE, "reciprocal", tin[:, 1:2], tin[:, 0:1])
        b.bar(DVE)
        b.do(DVE, "tensor_scalar", comb_sb[:, sc * NE: sc * NE + 8], mrt[:, :], tin[:, 1:2], None,
             ALU.mult, inc=(sm["dve"], 1))
        ext_free = sm["dve"].n
        b.wait(DVE, sm["pe"], tp5b_done)
        b.do(DVE, "tensor_copy", cb128[:, 0:9], comb_sb[:, sc * NE: sc * NE + 9], inc=(sm["dve"], 1))
        t_cb = sm["dve"].n
        b.wait(PE, sm["dve"], t_cb)
        b.do(PE, "transpose", psT5b[:, :], cb128[:, :], ident_b[:, :], inc=(sm["pe"], 1))
        tp5b_done = sm["pe"].n
        b.wait(DVE, sm["pe"], tp5b_done)
        b.do(DVE, "tensor_copy", combT_sb[0:16, sc * 128:(sc + 1) * 128], psT5b[0:16, :],
             inc=(sm["dve"], 1))
    t_comb = sm["dve"].n
    b.close_scope("p5p", "p5")

    if stage == "comb":
        b.wait(SYNC, sm["dve"], t_comb)
        b.do(SYNC, "dma_start", out=dbg[:, 0:NT * NE], in_=comb_sb[:, :], inc=(sm["dbgst"], 16))
        return _finish(b, sm)

    w2g_sb = [A("p6", "s", f"w2g_sb{i}", [128, (FC // G2) * 1024], BF16) for i in range(2)]
    w1_sb = [A("p6", "s", f"w1_sb{i}", [128, DC * 128], BF16) for i in range(4)]
    acc_sb = A("p6", "s", "acc_sb", [128, 8 * D], F32)
    b1_sb = A("p6", "s", "b1_sb", [128, NE * FC], F32)
    B2_sb = A("p6", "s", "B2_sb", [9, D], BF16)
    x6b = [A("p6", "s", f"x6b{i}", [128, D], F32) for i in range(2)]
    scr6 = A("p6", "s", "scr6", [128, D], F32)
    h1T_sb = A("p6r", "s", "h1T_sb", [128, FC * 1024], BF16, side="right")
    psM1 = [A("p6p", "p", f"psM1{i}", [128, D], F32) for i in range(2)]
    psM2 = [A("p6p", "p", f"psM2{i}", [128, D], F32) for i in range(2)]

    b.wait(POOL, sm["dve"], t_comb)
    b.wait(SYNC, sm["dve"], t_comb)
    b.do(POOL, "dma_start", out=b1_sb[:, :].rearrange("p (e fc) -> p e fc", fc=FC),
         in_=b1t[:, :, :].transpose([1, 0, 2]), inc=(sm["bload"], 16))
    b.do(POOL, "dma_start", out=B2_sb[:, :], in_=B2in[:, :], inc=(sm["bload"], 16))
    t_bload = sm["bload"].n

    w1_tok = [0, 0, 0, 0]
    w1_free = [0, 0, 0, 0]
    w2_tok = [0, 0]
    w2_free = [0, 0]
    m1ps_free = [0, 0]
    m2ps_free = [0, 0]
    gelu_tok = {}
    x6_free = [0, 0]

    def load_w1(e, fc, par):
        b.wait(POOL, sm["act"], w1_free[par])
        b.do(POOL, "dma_start",
             out=w1_sb[par][:, :].rearrange("p (dc m) -> p dc m", m=128),
             in_=w1_ch[e, fc].rearrange("(dc p) m -> p dc m", p=128),
             inc=(ss_w1[par], 16))
        w1_tok[par] = ss_w1[par].n

    def load_w2(e, g, par):
        b.wait(POOL, sm["pe"], w2_free[par])
        rows = (FC // G2) * 128
        b.do(POOL, "dma_start",
             out=w2g_sb[par][:, :].rearrange("p (fc n) -> p fc n", n=D),
             in_=w2T[e, g * rows:(g + 1) * rows, :].rearrange("(fc p) n -> p fc n", p=128),
             inc=(ss_w2[par], 16))
        w2_tok[par] = ss_w2[par].n

    b.wait(PE, sm["dve"], t_comb)
    b.wait(PE, sm["bload"], t_bload)
    b.wait(ACT, sm["bload"], t_bload)

    fcg = 0
    w2i = 0

    def emit_b2_init(half):
        nonlocal_m2 = None
        for st in range(8):
            sc = half * 8 + st
            parp = st % 2
            b.wait(PE, sm["dve"], m2ps_free[parp])
            b.do(PE, "matmul", psM2[parp][:, 0:512], combT_sb[0:9, sc * 128:(sc + 1) * 128],
                 B2_sb[0:9, 0:512], start=True, stop=True)
            b.do(PE, "matmul", psM2[parp][:, 512:1024], combT_sb[0:9, sc * 128:(sc + 1) * 128],
                 B2_sb[0:9, 512:1024], start=True, stop=True, inc=(sm["pe"], 1))
            b.wait(DVE, sm["pe"], sm["pe"].n)
            b.do(DVE, "tensor_copy", acc_sb[:, st * D:(st + 1) * D], psM2[parp][:, :],
                 inc=(sm["dve"], 1), plain=True)
            m2ps_free[parp] = sm["dve"].n

    for half in range(2):
        for e in range(NE):
            for fc in range(FC):
                par = fcg % 4
                fcg += 1
                parp1 = fc % 2
                load_w1(e, fc, par)
                b.wait(PE, ss_w1[par], w1_tok[par])
                b.wait(PE, sm["act"], m1ps_free[parp1])
                for dcd in range(DC):
                    for nn in range(2):
                        b.do(PE, "matmul",
                             psM1[parp1][:, nn * 512:(nn + 1) * 512],
                             w1_sb[par][:, dcd * 128:(dcd + 1) * 128],
                             h2T_view[:, dcd, half * 1024 + nn * 512: half * 1024 + (nn + 1) * 512],
                             start=(dcd == 0), stop=(dcd == DC - 1),
                             inc=(sm["pe"], 1) if (dcd == DC - 1 and nn == 1) else None)
                t_m1 = sm["pe"].n
                b.wait(ACT, sm["pe"], t_m1)
                lastfc = (fc >= FC - 2)
                b.do(ACT, "activation", out=h1T_sb[:, fc * 1024:(fc + 1) * 1024],
                     in_=psM1[parp1][:, :], func=AF.Gelu_apprx_tanh,
                     bias=b1_sb[:, e * FC + fc: e * FC + fc + 1], inc=(sm["act"], 1),
                     plain=not lastfc)
                gelu_tok[fc] = sm["act"].n + (0 if lastfc else 2)
                m1ps_free[parp1] = sm["act"].n
                w1_free[par] = sm["act"].n
            if e == 0:
                emit_b2_init(half)
            load_w2(e, 0, w2i % 2)
            for g in range(G2):
                parw = w2i % 2
                w2i += 1
                if g + 1 < G2:
                    load_w2(e, g + 1, w2i % 2)
                b.wait(PE, ss_w2[parw], w2_tok[parw])
                npg = FC // G2
                for st in range(8):
                    sc = half * 8 + st
                    parp = st % 2
                    b.wait(PE, sm["dve"], m2ps_free[parp])
                    for i in range(npg):
                        fc = g * npg + i
                        if st == 0:
                            b.wait(PE, sm["act"], gelu_tok[fc])
                        for nn in range(2):
                            b.do(PE, "matmul",
                                 psM2[parp][:, nn * 512:(nn + 1) * 512],
                                 h1T_sb[:, fc * 1024 + st * 128: fc * 1024 + (st + 1) * 128],
                                 w2g_sb[parw][:, i * 1024 + nn * 512: i * 1024 + (nn + 1) * 512],
                                 start=(i == 0), stop=(i == npg - 1),
                                 inc=(sm["pe"], 1) if (i == npg - 1 and nn == 1) else None)
                    t_m2 = sm["pe"].n
                    b.wait(DVE, sm["pe"], t_m2)
                    b.do(DVE, "scalar_tensor_tensor", acc_sb[:, st * D:(st + 1) * D],
                         psM2[parp][:, :], comb_sb[:, sc * NE + e: sc * NE + e + 1],
                         acc_sb[:, st * D:(st + 1) * D], ALU.mult, ALU.add, inc=(sm["dve"], 1),
                         plain=True)
                    m2ps_free[parp] = sm["dve"].n
                w2_free[parw] = sm["pe"].n
        for st in range(8):
            sc = half * 8 + st
            par = st % 2
            b.wait(SYNC, ss_oo[par], x6_free[par])
            b.wait(SYNC, ss_x1o[0], t_x1_all[0])
            b.wait(SYNC, ss_x1o[1], t_x1_all[1])
            b.do(SYNC, "dma_start", out=x6b[par][:, :], in_=x1d[sc * 128:(sc + 1) * 128, :],
                 inc=(ss_x6[par], 16))
            t_xl = ss_x6[par].n
            b.do(DVE, "tensor_mul", scr6[:, :], acc_sb[:, st * D:(st + 1) * D], gb_sb[:, :])
            b.wait(DVE, ss_x6[par], t_xl)
            b.do(DVE, "tensor_add", x6b[par][:, :], x6b[par][:, :], scr6[:, :], inc=(sm["dve"], 1))
            t_o = sm["dve"].n
            b.wait(SYNC, sm["dve"], t_o)
            b.do(SYNC, "dma_start", out=out[sc * 128:(sc + 1) * 128, :], in_=x6b[par][:, :],
                 inc=(ss_oo[par], 16))
            x6_free[par] = ss_oo[par].n

    return _finish(b, sm)


def _finish(b, sm):
    for s in b.final_waits:
        b.wait(SYNC, s, s.n)
    if sm["outst"].n:
        b.wait(SYNC, sm["outst"], sm["outst"].n)
    if sm["dbgst"].n:
        eng = POOL if b.ops[POOL] else SYNC
        b.wait(eng, sm["dbgst"], sm["dbgst"].n)
    b.flush()
    b.stack.close()
    for scn in reversed(list(b.scopes)):
        b.scopes.pop(scn).close()
    return b.nc


def _rel_buckets_1d():
    NUM_BUCKETS, MAX_DISTANCE = 32, 128
    half = NUM_BUCKETS // 2
    max_exact = half // 2
    coeff = (half - max_exact) / math.log(MAX_DISTANCE / max_exact)
    d = np.arange(-2047, 2048)
    arel = np.abs(d)
    base = np.where(d > 0, half, 0)
    distant = max_exact + (np.log(np.maximum(arel, 1).astype(np.float32) / max_exact) * coeff).astype(np.int32)
    distant = np.minimum(distant, half - 1)
    return base + np.where(arel < max_exact, arel, distant)


def host_layout(inputs):
    qkv_wT = np.ascontiguousarray(inputs["L0_attn_qkv_w"].T)
    qkvT_ch = np.ascontiguousarray(qkv_wT.reshape(D, 24, 128).transpose(1, 0, 2))
    owT = np.ascontiguousarray(inputs["L0_attn_o_w"].T)
    wmod_ch = np.ascontiguousarray(inputs["L0_wmod"].reshape(D, 12, 512).transpose(1, 0, 2))
    rwT = np.ascontiguousarray(inputs["L0_ffn_router_w"].T)
    rwT_ch = np.ascontiguousarray(rwT.reshape(8, 128, 8))
    rbv = np.ascontiguousarray(inputs["L0_ffn_router_b"].reshape(8, 1))
    w1 = np.concatenate([inputs["L0_ffn_spec_w1"], inputs["L0_ffn_shared_w1"][None]], axis=0)
    w1T = w1.transpose(0, 2, 1)
    w1_ch = np.ascontiguousarray(w1T.reshape(NE, D, FC, 128).transpose(0, 2, 1, 3))
    w2 = np.concatenate([inputs["L0_ffn_spec_w2"], inputs["L0_ffn_shared_w2"][None]], axis=0)
    w2Tv = np.ascontiguousarray(w2.transpose(0, 2, 1))
    b1 = np.concatenate([inputs["L0_ffn_spec_b1"], inputs["L0_ffn_shared_b1"][None]], axis=0)
    b1tv = np.ascontiguousarray(b1.reshape(NE, FC, 128).transpose(0, 2, 1))
    B2 = np.ascontiguousarray(
        np.concatenate([inputs["L0_ffn_spec_b2"], inputs["L0_ffn_shared_b2"][None]], axis=0))
    buckets = _rel_buckets_1d()
    tab = inputs["L0_attn_emb_weight"][buckets, :]
    tabr = tab[::-1]
    Tpv = np.zeros((NH, 128, EB), np.float32)
    for p in range(128):
        Tpv[:, p, p + 1:] = tabr[0:EB - 1 - p, :].T
    return dict(qkvT_ch=qkvT_ch, owT=owT, wmod_ch=wmod_ch, rwT_ch=rwT_ch, rb=rbv,
                w1_ch=w1_ch, w2T=w2Tv, b1t=b1tv, B2in=B2, Tp=Tpv,
                identf=np.eye(128, dtype=np.float32))


def core_inputs(inputs, shared, bidx):
    d = dict(shared)
    d["x"] = np.ascontiguousarray(inputs["L0_attn_x"][bidx])
    d["cvec"] = np.ascontiguousarray(inputs["L0_c"][bidx].reshape(1, D))
    return d


def kernel(**inputs):
    inputs = {k: np.asarray(v) for k, v in inputs.items()}
    shared = host_layout(inputs)
    nc = build_program("full")
    in_maps = [core_inputs(inputs, shared, bi) for bi in range(8)]
    res = run_bass_kernel_spmd(nc, in_maps, core_ids=list(range(8)))
    return np.stack([np.asarray(r["out"]) for r in res.results], axis=0)


# version 10 (final)
# speedup vs baseline: 1.0957x; 1.0957x over previous
import math
from contextlib import ExitStack

import numpy as np
import ml_dtypes
import concourse.bass as bass
import concourse.mybir as mybir
from concourse.ap import AP
from concourse.bass_utils import run_bass_kernel_spmd

F32 = mybir.dt.float32
BF16 = mybir.dt.bfloat16
AF = mybir.ActivationFunctionType
ALU = mybir.AluOpType
AX = mybir.AxisListType

D = 1024
S = 2048
NT = 16
NH = 16
HP = 8
KT = 16
DC = 8
DFF = 4096
FC = 32
NE = 9
G2 = 4
EB = 4096

ENGINES = ("sync", "tensor", "scalar", "vector", "gpsimd")
FLUSH_PER_PHASE = True
PE, ACT, DVE, SYNC, POOL = "tensor", "scalar", "vector", "sync", "gpsimd"


class Sem:
    __slots__ = ("h", "n")

    def __init__(self, h):
        self.h = h
        self.n = 0


class Builder:
    def __init__(self):
        self.nc = bass.Bass("TRN2", target_bir_lowering=False)
        self.stack = ExitStack()
        self.ops = {k: [] for k in ENGINES}
        self.scopes = {}
        self.block = None
        self.final_waits = []

    def E(self, eng, fn):
        self.ops[eng].append(fn)

    def wait(self, eng, sem, val):
        if val:
            self.E(eng, lambda e, h=sem.h, v=val: e.wait_ge(h, v))

    def do(self, eng, meth, *a, inc=None, lp=False, plain=False, **kw):
        incs = []
        if inc is not None:
            incs = inc if isinstance(inc, list) else [inc]
        for s, amt in incs:
            s.n += amt
        caps = [(s.h, amt) for s, amt in incs]
        if lp:
            nc = self.nc

            def fn(e, meth=meth, a=a, kw=kw, caps=caps, plain=plain):
                with nc.allow_low_precision(reason="intentional bf16 output"):
                    ins = getattr(e, meth)(*a, **kw)
                if not plain:
                    ins = e.drain()
                for h, amt in caps:
                    ins = ins.then_inc(h, amt)
            self.E(eng, fn)
            return incs[0][0].n if incs else None

        if eng in ("vector", "scalar") and caps and not plain:
            def fn(e, meth=meth, a=a, kw=kw, caps=caps):
                getattr(e, meth)(*a, **kw)
                ins = e.drain()
                for h, amt in caps:
                    ins = ins.then_inc(h, amt)
        else:
            def fn(e, meth=meth, a=a, kw=kw, caps=caps):
                ins = getattr(e, meth)(*a, **kw)
                for h, amt in caps:
                    ins = ins.then_inc(h, amt)
        self.E(eng, fn)
        return incs[0][0].n if incs else None

    def bar(self, eng):
        self.E(eng, lambda e: e.drain())

    def sem(self, name):
        h = self.stack.enter_context(self.nc.semaphore(name))
        return Sem(h)

    def flush(self):
        ops = self.ops
        self.ops = {k: [] for k in ENGINES}
        for name in ENGINES:
            if ops[name]:
                getattr(self.block, name)(lambda e, o=ops[name]: [f(e) for f in o])

    def alloc(self, scope, kind, name, shape, dtype, side="left"):
        if scope not in self.scopes:
            self.scopes[scope] = ExitStack()
        if kind == "s":
            return self.scopes[scope].enter_context(
                self.nc.sbuf_tensor(name, shape, dtype, side=side))
        return self.scopes[scope].enter_context(self.nc.psum_tensor(name, shape, dtype))

    def close_scope(self, *names):
        if FLUSH_PER_PHASE:
            self.flush()
        for n in names:
            if n in self.scopes:
                self.scopes.pop(n).close()


def build_program(stage="full", serial_attn=False):
    b = Builder()
    nc = b.nc

    def din(name, shape, dtype=F32):
        return nc.dram_tensor(name, shape, dtype, kind="ExternalInput").ap()

    x_in = din("x", [S, D])
    cvec = din("cvec", [1, D])
    wmod_ch = din("wmod_ch", [12, D, 512])
    qkvT_ch = din("qkvT_ch", [24, D, 128])
    owT = din("owT", [D, D])
    Tp = din("Tp", [NH, 128, EB])
    rwT_ch = din("rwT_ch", [8, 128, 8])
    rb = din("rb", [8, 1])
    w1_ch = din("w1_ch", [NE, FC, D, 128])
    w2T = din("w2T", [NE, DFF, D])
    b1t = din("b1t", [NE, 128, FC])
    B2in = din("B2in", [NE, D])
    identf = din("identf", [128, 128], F32)

    out = nc.dram_tensor("out", [S, D], F32, kind="ExternalOutput").ap()
    x1d = nc.dram_tensor("x1d", [S, D], F32, kind="Internal").ap()
    aTd = nc.dram_tensor("aTd", [NH, 65, S], BF16, kind="Internal").ap()
    dbg = None
    if stage != "full":
        dbg = nc.dram_tensor("dbg", [128, DC * S], F32, kind="ExternalOutput").ap()

    sm = {n: b.sem(n) for n in [
        "xld", "wm", "qw", "kw", "vw", "ow", "rw", "w1", "w2", "bload",
        "Tld", "Eld", "Est", "x1st", "x1ld", "outst", "dbgst", "aTst", "aTld",
        "pe", "act", "dve",
    ]}
    slot_sems = {}

    def ssem(name, nslots):
        slot_sems[name] = [b.sem(f"{name}_s{i}") for i in range(nslots)]
        return slot_sems[name]

    ss_wm = ssem("wm2", 2)
    ss_qw = ssem("qw2", 2)
    ss_kw = ssem("kw2", 2)
    ss_tb = ssem("tb2", 2)
    ss_w1 = ssem("w1r", 4)
    ss_w2 = ssem("w2r", 2)
    ss_x = ssem("xr", 2)
    ss_x4 = ssem("x4r", 2)
    ss_x6 = ssem("x6r", 2)
    ss_ar = ssem("arr", 4)
    ss_sp = ssem("spr", 2)
    ss_x1o = ssem("x1o", 2)
    ss_oo = ssem("oor", 2)
    b.final_waits = [ss_oo[0], ss_oo[1]]
    b.block = b.stack.enter_context(nc.Block())

    A = b.alloc
    ident_b = A("static", "s", "ident_b", [128, 128], BF16)
    ident_f = A("static", "s", "ident_f", [128, 128], F32)
    onesf = A("static", "s", "onesf", [1, 64], BF16)
    zer128 = A("static", "s", "zer128", [128, 128], F32)
    neg16 = A("static", "s", "neg16", [128, 1], F32)
    comb_sb = A("static", "s", "comb_sb", [128, NT * NE], F32)
    combT_sb = A("static", "s", "combT_sb", [16, S], BF16)
    logits_sb = A("static", "s", "logits_sb", [128, NT * 8], F32)
    gb_sb = A("static", "s", "gb_sb", [128, D], F32)
    rb_sb = A("static", "s", "rb_sb", [8, 1], F32)
    h2T_sb = A("h2T", "s", "h2T_sb", [128, DC * S], BF16)
    mga = A("modv", "s", "mga", [128, D], F32)
    mcb = A("modv", "s", "mcb", [128, D], F32)
    msb = A("modv", "s", "msb", [128, D], F32)
    hT_sb = A("hT", "s", "hT_sb", [128, DC * S], BF16, side="right")
    Vhat_sb = A("vhat", "s", "Vhat_sb", [128, NT * 1040], BF16, side="right")
    Vw_sb = A("p2", "s", "Vw_sb", [128, DC * 1024], BF16, side="right")
    msa = A("modv2", "s", "msa", [128, D], F32, side="right")
    mca = A("modv2", "s", "mca", [128, D], F32, side="right")

    hT_view = hT_sb[:, :].rearrange("p (dc s) -> p dc s", s=S)
    h2T_view = h2T_sb[:, :].rearrange("p (dc s) -> p dc s", s=S)
    mod_t = {"sa": msa, "ca": mca, "ga": mga, "sb": msb, "cb": mcb, "gb": gb_sb}
    mod_order = ["sa", "ca", "ga", "sb", "cb", "gb"]

    b.do(POOL, "dma_start", out=ident_b[:, :], in_=identf[:, :], inc=(sm["Tld"], 16))
    b.do(SYNC, "dma_start", out=ident_f[:, :], in_=identf[:, :], inc=(sm["Tld"], 16))
    b.do(SYNC, "dma_start", out=rb_sb[:, :], in_=rb[:, :], inc=(sm["Tld"], 16))
    t_consts = sm["Tld"].n
    b.wait(DVE, sm["Tld"], t_consts)
    b.do(DVE, "memset", onesf[:, :], 1.0)
    b.do(DVE, "memset", neg16[:, :], -16.0)
    b.do(DVE, "memset", zer128[:, :], 0.0, inc=(sm["dve"], 1))
    t_zer = sm["dve"].n

    ct = A("p0", "s", "ct", [128, 8], F32, side="right")
    sct = A("p0", "s", "sct", [128, 8], F32, side="right")
    srep = A("p0", "s", "srep", [128, 8 * 128], BF16, side="right")
    wm_sb = [A("p0", "s", f"wm_sb{i}", [128, 8 * 512], BF16, side="right") for i in range(2)]
    scr1k = A("p0", "s", "scr1k", [128, D], F32, side="right")
    scr1b = A("p0", "s", "scr1b", [128, D], F32, side="right")
    psMod = [A("p0p", "p", "psMod0", [128, 512], F32)] * 2

    for dc in range(DC):
        b.do(SYNC, "dma_start", out=ct[:, dc:dc + 1],
             in_=AP(tensor=cvec.tensor, offset=dc * 128, ap=[[1, 128], [1, 1]]),
             inc=(sm["xld"], 16))
    t_ct = sm["xld"].n
    b.wait(ACT, sm["xld"], t_ct)
    b.do(ACT, "activation", out=sct[:, :], in_=ct[:, :], func=AF.Silu)
    b.bar(ACT)
    b.wait(ACT, sm["dve"], t_zer)
    for dc in range(DC):
        b.do(ACT, "activation", out=srep[:, dc * 128:(dc + 1) * 128], in_=zer128[:, :],
             func=AF.Identity, bias=sct[:, dc:dc + 1],
             inc=(sm["act"], 1) if dc == DC - 1 else None)
    t_srep = sm["act"].n
    b.wait(PE, sm["act"], t_srep)

    wm_free = [0, 0]
    ps_free = [0, 0]
    for n in range(12):
        par = n % 2
        vec = mod_order[n // 2]
        col = (n % 2) * 512
        b.wait(POOL, sm["pe"], wm_free[par])
        b.do(POOL, "dma_start",
             out=wm_sb[par][:, :].rearrange("p (dc j) -> p dc j", j=512),
             in_=wmod_ch[n].rearrange("(dc p) j -> p dc j", p=128),
             inc=(ss_wm[par], 16))
        b.wait(PE, ss_wm[par], ss_wm[par].n)
        b.wait(PE, sm["dve"], max(ps_free))
        for dc in range(DC):
            b.do(PE, "matmul", psMod[par][:, :], srep[:, dc * 128:(dc + 1) * 128],
                 wm_sb[par][:, dc * 512:(dc + 1) * 512],
                 start=(dc == 0), stop=(dc == DC - 1),
                 inc=(sm["pe"], 1) if dc == DC - 1 else None)
        wm_free[par] = sm["pe"].n
        b.wait(DVE, sm["pe"], sm["pe"].n)
        b.do(DVE, "tensor_copy", mod_t[vec][:, col:col + 512], psMod[par][:, :], inc=(sm["dve"], 1))
        ps_free[par] = sm["dve"].n
    b.do(DVE, "tensor_scalar", mca[:, :], mca[:, :], 1.0, None, ALU.add)
    b.do(DVE, "tensor_scalar", mcb[:, :], mcb[:, :], 1.0, None, ALU.add)
    for g in (mga, gb_sb):
        b.do(DVE, "tensor_scalar", scr1k[:, :], g[:, :], 0.0, None, ALU.is_ge)
        b.do(DVE, "tensor_scalar", scr1k[:, :], scr1k[:, :], 2.0, -1.0, ALU.mult, ALU.add)
        b.do(DVE, "tensor_scalar", scr1b[:, :], g[:, :], -1.0, None, ALU.mult)
        b.do(DVE, "tensor_max", g[:, :], g[:, :], scr1b[:, :])
        b.do(DVE, "tensor_scalar", g[:, :], g[:, :], 0.25, None, ALU.max)
        b.do(DVE, "tensor_mul", g[:, :], g[:, :], scr1k[:, :])
    b.do(DVE, "tensor_copy", scr1k[:, 0:1], scr1k[:, 0:1], inc=(sm["dve"], 1))
    t_mod = sm["dve"].n
    for dcd in range(DC):
        for j in range(8):
            b.do(POOL, "dma_start",
                 out=Vw_sb[:, dcd * 1024 + j * 128: dcd * 1024 + (j + 1) * 128],
                 in_=qkvT_ch[16 + j][dcd * 128:(dcd + 1) * 128, :],
                 inc=(sm["vw"], 16))
    t_vw = sm["vw"].n

    xb = [A("p1", "s", f"xb{i}", [128, D], F32, side="right") for i in range(2)]
    hbuf = [A("p1", "s", f"hbuf{i}", [128, D], BF16, side="right") for i in range(2)]
    stat = A("p1", "s", "stat", [128, 8], F32, side="right")
    psSq = A("p1p", "p", "psSq", [128, D], F32)
    psT = A("p1p", "p", "psT", [128, D], BF16)
    psV = [A("p1p", "p", f"psV{i}", [128, D], F32) for i in range(2)]

    def emit_ln(src_tile, cav, sav, hout, scr, st8, psq, xc=None):
        b.do(DVE, "reduce_sum", st8[:, 0:1], src_tile, axis=AX.X)
        b.bar(DVE)
        b.do(DVE, "tensor_scalar", st8[:, 1:2], st8[:, 0:1], -1.0 / D, None, ALU.mult)
        b.bar(DVE)
        dst = src_tile if xc is None else xc
        b.do(DVE, "tensor_scalar", dst, src_tile, st8[:, 1:2], None, ALU.add, inc=(sm["dve"], 1))
        t1 = sm["dve"].n
        b.wait(ACT, sm["dve"], t1)
        b.do(ACT, "activation", out=psq[:, :], in_=dst, func=AF.Square,
             accum_out=st8[:, 2:3], inc=(sm["act"], 1))
        t2 = sm["act"].n
        b.wait(DVE, sm["act"], t2)
        b.do(DVE, "tensor_scalar", st8[:, 3:4], st8[:, 2:3], 1.0 / D, 1e-5, ALU.mult, ALU.add)
        b.bar(DVE)
        b.do(DVE, "reciprocal", st8[:, 4:5], st8[:, 3:4], inc=(sm["dve"], 1))
        t3 = sm["dve"].n
        b.wait(ACT, sm["dve"], t3)
        b.do(ACT, "activation", out=st8[:, 5:6], in_=st8[:, 4:5], func=AF.Sqrt, inc=(sm["act"], 1))
        t4 = sm["act"].n
        b.do(DVE, "tensor_mul", scr, dst, cav[:, :])
        b.wait(DVE, sm["act"], t4)
        b.do(DVE, "scalar_tensor_tensor", hout, scr, st8[:, 5:6],
             sav[:, :], ALU.mult, ALU.add, inc=(sm["dve"], 1))
        return sm["dve"].n

    vones = Vhat_sb[:, :].rearrange("p (s hd) -> p s hd", hd=1040)[:, :, :].rearrange(
        "p s (h d) -> p s h d", d=65)[:, :, :, 64:65]
    b.do(DVE, "memset", vones, 1.0, inc=(sm["dve"], 1))
    t_vones = sm["dve"].n
    b.wait(DVE, sm["dve"], t_mod)
    b.wait(PE, sm["vw"], t_vw)
    x_free = [0, 0]
    h_free = [0, 0]
    psv_free = [0, 0]
    tp_ev = 0
    for sc in range(NT):
        par = sc % 2
        b.wait(SYNC, sm["dve"], x_free[par])
        b.do(SYNC, "dma_start", out=xb[par][:, :], in_=x_in[sc * 128:(sc + 1) * 128, :],
             inc=(ss_x[par], 16))
        b.wait(DVE, ss_x[par], ss_x[par].n)
        b.wait(DVE, sm["pe"], h_free[par])
        t_h = emit_ln(xb[par][:, :], mca, msa, hbuf[par][:, :], scr1k[:, :], stat, psSq)
        x_free[par] = t_h
        b.wait(PE, sm["dve"], t_h)
        b.wait(PE, sm["dve"], tp_ev)
        for dc in range(DC):
            b.do(PE, "transpose", psT[:, dc * 128:(dc + 1) * 128], hbuf[par][:, dc * 128:(dc + 1) * 128],
                 ident_b[:, :], inc=(sm["pe"], 1) if dc == DC - 1 else None)
        h_free[par] = sm["pe"].n
        b.wait(DVE, sm["pe"], sm["pe"].n)
        b.do(DVE, "tensor_copy", hT_view[:, :, sc * 128:(sc + 1) * 128],
             psT[:, :].rearrange("p (dc j) -> p dc j", j=128), inc=(sm["dve"], 1))
        tp_ev = sm["dve"].n
        b.wait(PE, sm["dve"], tp_ev)
        b.wait(PE, sm["dve"], psv_free[par])
        for dcd in range(DC):
            for nn in range(2):
                b.do(PE, "matmul", psV[par][:, nn * 512:(nn + 1) * 512],
                     hT_view[:, dcd, sc * 128:(sc + 1) * 128],
                     Vw_sb[:, dcd * 1024 + nn * 512: dcd * 1024 + (nn + 1) * 512],
                     start=(dcd == 0), stop=(dcd == DC - 1),
                     inc=(sm["pe"], 1) if (dcd == DC - 1 and nn == 1) else None)
        b.wait(DVE, sm["pe"], sm["pe"].n)
        if sc == 0:
            b.wait(DVE, sm["dve"], t_vones)
        vview = Vhat_sb[:, sc * 1040:(sc + 1) * 1040].rearrange("p (h d) -> p h d", d=65)[:, :, 0:64]
        b.do(DVE, "tensor_copy", vview, psV[par][:, :].rearrange("p (h d) -> p h d", d=64),
             inc=(sm["dve"], 1))
        psv_free[par] = sm["dve"].n
    t_hT = sm["dve"].n
    t_vhat = sm["dve"].n
    t_vpe = sm["pe"].n
    b.close_scope("p1p", "p0p", "p1", "p0", "modv2", "p2")

    if stage == "hT":
        b.wait(POOL, sm["dve"], t_hT)
        b.do(POOL, "dma_start", out=dbg[:, :], in_=hT_sb[:, :], inc=(sm["dbgst"], 16))
        return _finish(b, sm)

    qw_sb = [A("attn", "s", f"qw_sb{i}", [128, DC * 128], BF16, side="right") for i in range(2)]
    kw_sb = [A("attn", "s", f"kw_sb{i}", [128, DC * 128], BF16, side="right") for i in range(2)]
    qT_sb = [A("attn", "s", f"qT_sb{i}", [128, S], BF16, side="right") for i in range(2)]
    kT_sb = [A("attn", "s", f"kT_sb{i}", [128, S], BF16, side="right") for i in range(2)]
    Tb = [A("attn", "s", f"Tb{i}", [128, EB], BF16, side="right") for i in range(2)]
    ptb = [A("attn", "s", f"ptb{i}", [128, S], BF16, side="right") for i in range(4)]
    aT_pair = [A("attnL", "s", f"aT_pair{i}", [128, S], BF16) for i in range(2)]
    psQK = A("attnp", "p", "psQK", [128, S], F32)
    psaT = A("attnp", "p", "psaT", [128, S], F32)

    b.wait(SYNC, sm["dve"], t_vhat)
    b.wait(POOL, sm["dve"], t_vhat)

    Tb_free = [0, 0]
    Tb_tok = {}

    def load_Tb(h):
        par = h % 2
        b.wait(POOL, sm["pe"], Tb_free[par])
        b.do(POOL, "dma_start", out=Tb[par][:, :], in_=Tp[h], inc=(ss_tb[par], 16))
        Tb_tok[h] = (ss_tb[par], ss_tb[par].n)

    qkw_free = [0, 0]
    qkw_tok = {}

    def load_qk_w(hp):
        par = hp % 2
        b.wait(POOL, sm["pe"], qkw_free[par])
        for dcd in range(DC):
            b.do(POOL, "dma_start", out=qw_sb[par][:, dcd * 128:(dcd + 1) * 128],
                 in_=qkvT_ch[hp][dcd * 128:(dcd + 1) * 128, :], inc=(ss_qw[par], 16))
            b.do(POOL, "dma_start", out=kw_sb[par][:, dcd * 128:(dcd + 1) * 128],
                 in_=qkvT_ch[8 + hp][dcd * 128:(dcd + 1) * 128, :], inc=(ss_kw[par], 16))
        qkw_tok[hp] = (ss_qw[par].n, ss_kw[par].n)

    load_qk_w(0)
    load_qk_w(1)
    load_Tb(0)
    load_Tb(1)

    qkT_free = [0, 0]
    psQK_free = 0
    psaT_free = 0
    aTp_free = [0, 0]
    ptb_free = [0] * 4

    for hp in range(HP):
        par = hp % 2
        b.wait(PE, ss_qw[hp % 2], qkw_tok[hp][0])
        b.wait(PE, ss_kw[hp % 2], qkw_tok[hp][1])
        b.wait(PE, sm["pe"], qkT_free[par])
        b.wait(PE, sm["act"], psQK_free)
        if hp == 0:
            b.wait(PE, sm["dve"], t_vhat)
        for wi, (wsb, dst, scale) in enumerate(
                ((qw_sb[par], qT_sb[par], 1.0), (kw_sb[par], kT_sb[par], 0.125))):
            if wi == 1:
                b.wait(PE, sm["act"], sm["act"].n)
            for dcd in range(DC):
                for nn in range(4):
                    b.do(PE, "matmul", psQK[:, nn * 512:(nn + 1) * 512],
                         wsb[:, dcd * 128:(dcd + 1) * 128],
                         hT_view[:, dcd, nn * 512:(nn + 1) * 512],
                         start=(dcd == 0), stop=(dcd == DC - 1),
                         inc=(sm["pe"], 1) if (dcd == DC - 1 and nn == 3) else None)
            t_qk = sm["pe"].n
            b.wait(ACT, sm["pe"], t_qk)
            b.do(ACT, "activation", out=dst[:, :], in_=psQK[:, :], func=AF.Copy, scale=scale,
                 inc=(sm["act"], 1))
        qkw_free[par] = sm["pe"].n
        t_qkevac = sm["act"].n
        psQK_free = t_qkevac
        if hp + 1 < HP:
            load_qk_w(hp + 1)

        psQK_free_lo = psQK_free
        for lh in range(2):
            h = 2 * hp + lh
            base = 64 * lh
            b.wait(PE, sm["dve"], psaT_free)
            b.wait(PE, Tb_tok[h][0], Tb_tok[h][1])
            exp_u = {}
            pv_tok = {}

            def emit_unit(kt, qq):
                sl = (4 * kt + qq) % 4
                pslot = psQK[:, sl * 512:(sl + 1) * 512]
                if kt == 0:
                    b.wait(PE, sm["act"], psQK_free_lo if qq < 2 else psQK_free)
                if kt >= 1:
                    b.wait(PE, sm["act"], exp_u[(kt - 1, qq)])
                off = 2048 - 128 * kt + qq * 512
                b.do(PE, "matmul", pslot, ident_b[:, :], Tb[h % 2][:, off:off + 512],
                     start=True, stop=False)
                b.do(PE, "matmul", pslot,
                     kT_sb[par][base:base + 64, kt * 128:(kt + 1) * 128],
                     qT_sb[par][base:base + 64, qq * 512:(qq + 1) * 512],
                     start=False, stop=True, inc=(sm["pe"], 1))
                t_sc = sm["pe"].n
                drained = (kt == KT - 1 and qq >= 2)
                b.wait(ACT, sm["pe"], t_sc)
                if qq == 0:
                    b.wait(ACT, sm["pe"], ptb_free[kt % 4])
                b.do(ACT, "activation", out=ptb[kt % 4][:, qq * 512:(qq + 1) * 512],
                     in_=pslot, func=AF.Exp, bias=neg16[:, 0:1], inc=(sm["act"], 1),
                     plain=not drained)
                exp_u[(kt, qq)] = sm["act"].n

            def emit_pv(kt):
                lagtok = exp_u[(kt, 3)] + (0 if kt == KT - 1 else 2)
                b.wait(PE, sm["act"], lagtok)
                for nn in range(4):
                    b.do(PE, "matmul", psaT[0:65, nn * 512:(nn + 1) * 512],
                         Vhat_sb[:, kt * 1040 + 65 * h: kt * 1040 + 65 * h + 65],
                         ptb[kt % 4][:, nn * 512:(nn + 1) * 512],
                         start=(kt == 0), stop=(kt == KT - 1),
                         inc=(sm["pe"], 1) if nn == 3 else None)
                pv_tok[kt] = sm["pe"].n
                ptb_free[kt % 4] = sm["pe"].n

            for kt in range(KT):
                for qq in range(4):
                    emit_unit(kt, qq)
                if kt >= 1:
                    emit_pv(kt - 1)
            emit_pv(KT - 1)
            t_pv = sm["pe"].n
            psQK_free_lo = exp_u[(KT - 1, 3)]
            psQK_free = exp_u[(KT - 1, 3)]
            parh = h % 2
            b.wait(DVE, sm["pe"], t_pv)
            b.wait(DVE, ss_sp[parh], aTp_free[parh])
            b.do(DVE, "tensor_copy", aT_pair[parh][0:65, :], psaT[0:65, :], inc=(sm["dve"], 1),
                 lp=True)
            psaT_free = sm["dve"].n
            b.wait(SYNC, sm["dve"], psaT_free)
            b.do(SYNC, "dma_start", out=aTd[h], in_=aT_pair[parh][0:65, :],
                 inc=(ss_sp[parh], 16))
            aTp_free[parh] = ss_sp[parh].n
            if h + 2 < NH:
                Tb_free[h % 2] = t_pv
                load_Tb(h + 2)
        qkT_free[par] = sm["pe"].n

    t_attn = sm["dve"].n
    b.close_scope("attnp", "attnL", "attn", "vhat", "hT")


    aT4_sb = A("p4", "s", "aT4_sb", [128, DC * S], BF16, side="right")
    araw = [A("p4", "s", f"araw{i}", [65, S], BF16, side="right") for i in range(4)]
    ones2 = A("p4", "s", "ones2", [128, 64], BF16, side="right")
    recB = [A("p4", "s", f"recB{i}", [64, S], BF16, side="right") for i in range(2)]
    psN = [A("p4pre", "p", f"psN{i}", [64, S], F32) for i in range(2)]
    ow_sb = A("p4", "s", "ow_sb", [128, DC * 1024], BF16, side="right")
    x4b = [A("p4", "s", f"x4b{i}", [128, D], F32, side="right") for i in range(2)]
    x1b = [A("p4", "s", f"x1b{i}", [128, D], F32, side="right") for i in range(2)]
    h4buf = [A("p4", "s", f"h4buf{i}", [128, D], BF16, side="right") for i in range(2)]
    scr4 = A("p4", "s", "scr4", [128, D], F32, side="right")
    scr4b = A("p4", "s", "scr4b", [128, D], F32, side="right")
    stat4 = A("p4", "s", "stat4", [128, 8], F32, side="right")

    b.wait(POOL, sm["dve"], t_attn)
    b.wait(SYNC, sm["dve"], t_attn)
    for dcd in range(DC):
        b.do(POOL, "dma_start", out=ow_sb[:, dcd * 1024:(dcd + 1) * 1024],
             in_=owT[dcd * 128:(dcd + 1) * 128, :], inc=(sm["ow"], 16))
    t_ow = sm["ow"].n
    b.do(DVE, "memset", ones2[:, :], 1.0, inc=(sm["dve"], 1))
    t_ones2 = sm["dve"].n
    araw_free = [0, 0, 0, 0]
    psN_free = [0, 0]
    t_norm = 0
    for h in range(NH):
        parh = h % 4
        b.wait(SYNC, sm["dve"], araw_free[parh])
        b.do(SYNC, "dma_start", out=araw[parh][0:65, :], in_=aTd[h], inc=(ss_ar[parh], 16))
        t_al = ss_ar[parh].n
        b.wait(PE, ss_ar[parh], t_al)
        b.wait(PE, sm["dve"], psN_free[h % 2])
        if h == 0:
            b.wait(PE, sm["dve"], t_ones2)
        for nn in range(4):
            b.do(PE, "matmul", psN[h % 2][0:64, nn * 512:(nn + 1) * 512], ones2[64:65, 0:64],
                 araw[parh][64:65, nn * 512:(nn + 1) * 512], start=True, stop=True,
                 inc=(sm["pe"], 1) if nn == 3 else None)
        t_rb = sm["pe"].n
        b.wait(DVE, sm["pe"], t_rb)
        b.do(DVE, "reciprocal", recB[h % 2][:, :], psN[h % 2][0:64, :], lp=True, plain=True)
        b.do(DVE, "tensor_mul",
             aT4_sb[64 * (h % 2):64 * (h % 2) + 64, (h // 2) * 2048:(h // 2 + 1) * 2048],
             araw[parh][0:64, :], recB[h % 2][:, :], inc=(sm["dve"], 1),
             plain=(h < NH - 2))
        t_norm = sm["dve"].n
        psN_free[h % 2] = t_norm
        araw_free[parh] = t_norm
    b.close_scope("p4pre")
    psY = [A("p4p", "p", f"psY{i}", [128, D], F32) for i in range(2)]
    psT4 = A("p4p", "p", "psT4", [128, D], BF16)
    psSq4 = A("p4p", "p", "psSq4", [128, D], F32)
    b.wait(PE, sm["ow"], t_ow)
    b.wait(PE, sm["dve"], t_norm)

    psy_free = [0, 0]
    x4_free = [0, 0]
    x1_free = [0, 0]
    h4_free = [0, 0]
    tp4_ev = 0
    for sc in range(NT):
        par = sc % 2
        b.wait(PE, sm["dve"], psy_free[par])
        for hc in range(DC):
            for nn in range(2):
                b.do(PE, "matmul", psY[par][:, nn * 512:(nn + 1) * 512],
                     aT4_sb[:, hc * 2048 + sc * 128: hc * 2048 + (sc + 1) * 128],
                     ow_sb[:, hc * 1024 + nn * 512: hc * 1024 + (nn + 1) * 512],
                     start=(hc == 0), stop=(hc == DC - 1),
                     inc=(sm["pe"], 1) if (hc == DC - 1 and nn == 1) else None)
        t_oy = sm["pe"].n
        b.wait(SYNC, sm["dve"], x4_free[par])
        b.do(SYNC, "dma_start", out=x4b[par][:, :], in_=x_in[sc * 128:(sc + 1) * 128, :],
             inc=(ss_x4[par], 16))
        t_x = ss_x4[par].n
        b.wait(DVE, sm["pe"], t_oy)
        b.do(DVE, "tensor_mul", scr4[:, :], psY[par][:, :], mga[:, :], inc=(sm["dve"], 1))
        psy_free[par] = sm["dve"].n
        b.wait(DVE, ss_x4[par], t_x)
        b.wait(DVE, ss_x1o[par], x1_free[par])
        b.do(DVE, "tensor_add", x1b[par][:, :], x4b[par][:, :], scr4[:, :], inc=(sm["dve"], 1))
        t_x1add = sm["dve"].n
        x4_free[par] = t_x1add
        b.wait(SYNC, sm["dve"], t_x1add)
        b.do(SYNC, "dma_start", out=x1d[sc * 128:(sc + 1) * 128, :], in_=x1b[par][:, :],
             inc=(ss_x1o[par], 16))
        x1_free[par] = ss_x1o[par].n
        b.wait(DVE, sm["pe"], h4_free[par])
        t_h2 = emit_ln(x1b[par][:, :], mcb, msb, h4buf[par][:, :], scr4[:, :], stat4, psSq4,
                       xc=scr4b[:, :])
        b.wait(PE, sm["dve"], t_h2)
        b.wait(PE, sm["dve"], tp4_ev)
        for dcd in range(DC):
            b.do(PE, "transpose", psT4[:, dcd * 128:(dcd + 1) * 128],
                 h4buf[par][:, dcd * 128:(dcd + 1) * 128],
                 ident_b[:, :], inc=(sm["pe"], 1) if dcd == DC - 1 else None)
        h4_free[par] = sm["pe"].n
        b.wait(DVE, sm["pe"], sm["pe"].n)
        b.do(DVE, "tensor_copy", h2T_view[:, :, sc * 128:(sc + 1) * 128],
             psT4[:, :].rearrange("p (dc j) -> p dc j", j=128), inc=(sm["dve"], 1))
        tp4_ev = sm["dve"].n
    t_h2T = sm["dve"].n
    t_x1_all = (ss_x1o[0].n, ss_x1o[1].n)
    b.close_scope("p4p", "p4", "modv")

    if stage == "h2T":
        b.wait(POOL, sm["dve"], t_h2T)
        b.do(POOL, "dma_start", out=dbg[:, :], in_=h2T_sb[:, :], inc=(sm["dbgst"], 16))
        return _finish(b, sm)

    rw_sb = A("p5", "s", "rw_sb", [128, 64], BF16)
    lt_sbf = A("p5", "s", "lt_sbf", [128, S], F32)
    ext = A("p5", "s", "ext", [128, 8], F32)
    m8t = A("p5", "s", "m8t", [128, 8], F32)
    mvt = A("p5", "s", "mvt", [128, 8], F32)
    mrt = A("p5", "s", "mrt", [128, 8], F32)
    tin = A("p5", "s", "tin", [128, 8], F32)
    negm5 = A("p5", "s", "negm5", [128, 1], F32)
    cb128 = A("p5", "s", "cb128", [128, 128], BF16)
    psL = A("p5p", "p", "psL", [128, S], F32)
    psT5 = A("p5p", "p", "psT5", [128, 128], F32)
    psT5b = A("p5p", "p", "psT5b", [128, 128], BF16)

    b.wait(POOL, sm["dve"], t_h2T)
    b.do(POOL, "dma_start", out=rw_sb[:, :].rearrange("p (dc e) -> p dc e", e=8),
         in_=rwT_ch[:, :, :].transpose([1, 0, 2]), inc=(sm["rw"], 16))
    b.do(DVE, "memset", lt_sbf[:, :], 0.0)
    b.do(DVE, "memset", cb128[:, :], 0.0)
    combview = comb_sb[:, :].rearrange("p (sc e) -> p sc e", e=NE)[:, :, 8:9]
    b.do(DVE, "memset", combview, 1.0, inc=(sm["dve"], 1))
    t_msets = sm["dve"].n
    b.wait(PE, sm["rw"], sm["rw"].n)
    b.wait(PE, sm["dve"], t_h2T)
    b.wait(PE, sm["dve"], t_msets)
    for dcd in range(DC):
        for nn in range(4):
            b.do(PE, "matmul", psL[0:8, nn * 512:(nn + 1) * 512],
                 rw_sb[:, dcd * 8:(dcd + 1) * 8],
                 h2T_view[:, dcd, nn * 512:(nn + 1) * 512],
                 start=(dcd == 0), stop=(dcd == DC - 1),
                 inc=(sm["pe"], 1) if (dcd == DC - 1 and nn == 3) else None)
    t_l = sm["pe"].n
    b.wait(ACT, sm["pe"], t_l)
    b.wait(ACT, sm["Tld"], t_consts)
    b.do(ACT, "activation", out=lt_sbf[0:8, :], in_=psL[0:8, :], func=AF.Identity,
         bias=rb_sb[0:8, 0:1], inc=(sm["act"], 1))
    t_le = sm["act"].n
    b.wait(PE, sm["act"], t_le)
    tp5_ev = 0
    for sc in range(NT):
        b.wait(PE, sm["dve"], tp5_ev)
        b.do(PE, "transpose", psT5[:, :], lt_sbf[:, sc * 128:(sc + 1) * 128], ident_f[:, :],
             inc=(sm["pe"], 1))
        b.wait(DVE, sm["pe"], sm["pe"].n)
        b.do(DVE, "tensor_copy", logits_sb[:, sc * 8:(sc + 1) * 8], psT5[:, 0:8], inc=(sm["dve"], 1))
        tp5_ev = sm["dve"].n

    ext_free = 0
    tp5b_done = 0
    for sc in range(NT):
        lsl = logits_sb[:, sc * 8:(sc + 1) * 8]
        b.do(DVE, "reduce_max", negm5[:, 0:1], lsl, axis=AX.X, negate=True, inc=(sm["dve"], 1))
        t_nm = sm["dve"].n
        b.wait(ACT, sm["dve"], t_nm)
        b.wait(ACT, sm["dve"], ext_free)
        b.do(ACT, "activation", out=ext[:, :], in_=lsl, func=AF.Exp, bias=negm5[:, 0:1],
             inc=(sm["act"], 1))
        t_ex = sm["act"].n
        b.wait(DVE, sm["act"], t_ex)
        b.do(DVE, "max", m8t[:, :], ext[:, :])
        b.bar(DVE)
        b.do(DVE, "tensor_copy", mvt[:, 0:6], m8t[:, 2:8])
        b.do(DVE, "tensor_copy", mvt[:, 6:7], m8t[:, 7:8])
        b.do(DVE, "tensor_copy", mvt[:, 7:8], m8t[:, 7:8])
        b.bar(DVE)
        b.do(DVE, "match_replace", mrt[:, :], mvt[:, :], ext[:, :], 0.0)
        b.do(DVE, "tensor_scalar", tin[:, 0:1], m8t[:, 0:1], m8t[:, 1:2], None, ALU.add)
        b.bar(DVE)
        b.do(DVE, "reciprocal", tin[:, 1:2], tin[:, 0:1])
        b.bar(DVE)
        b.do(DVE, "tensor_scalar", comb_sb[:, sc * NE: sc * NE + 8], mrt[:, :], tin[:, 1:2], None,
             ALU.mult, inc=(sm["dve"], 1))
        ext_free = sm["dve"].n
        b.wait(DVE, sm["pe"], tp5b_done)
        b.do(DVE, "tensor_copy", cb128[:, 0:9], comb_sb[:, sc * NE: sc * NE + 9], inc=(sm["dve"], 1))
        t_cb = sm["dve"].n
        b.wait(PE, sm["dve"], t_cb)
        b.do(PE, "transpose", psT5b[:, :], cb128[:, :], ident_b[:, :], inc=(sm["pe"], 1))
        tp5b_done = sm["pe"].n
        b.wait(DVE, sm["pe"], tp5b_done)
        b.do(DVE, "tensor_copy", combT_sb[0:16, sc * 128:(sc + 1) * 128], psT5b[0:16, :],
             inc=(sm["dve"], 1))
    t_comb = sm["dve"].n
    b.close_scope("p5p", "p5")

    if stage == "comb":
        b.wait(SYNC, sm["dve"], t_comb)
        b.do(SYNC, "dma_start", out=dbg[:, 0:NT * NE], in_=comb_sb[:, :], inc=(sm["dbgst"], 16))
        return _finish(b, sm)

    w2g_sb = [A("p6", "s", f"w2g_sb{i}", [128, (FC // G2) * 1024], BF16) for i in range(2)]
    w1_sb = [A("p6", "s", f"w1_sb{i}", [128, DC * 128], BF16) for i in range(4)]
    acc_sb = A("p6", "s", "acc_sb", [128, 8 * D], F32)
    b1_sb = A("p6", "s", "b1_sb", [128, NE * FC], F32)
    B2_sb = A("p6", "s", "B2_sb", [9, D], BF16)
    x6b = [A("p6", "s", f"x6b{i}", [128, D], F32) for i in range(2)]
    scr6 = A("p6", "s", "scr6", [128, D], F32)
    h1T_sb = A("p6r", "s", "h1T_sb", [128, FC * 1024], BF16, side="right")
    psM1 = [A("p6p", "p", f"psM1{i}", [128, D], F32) for i in range(2)]
    psM2 = [A("p6p", "p", f"psM2{i}", [128, D], F32) for i in range(2)]

    b.wait(POOL, sm["dve"], t_comb)
    b.wait(SYNC, sm["dve"], t_comb)
    b.do(POOL, "dma_start", out=b1_sb[:, :].rearrange("p (e fc) -> p e fc", fc=FC),
         in_=b1t[:, :, :].transpose([1, 0, 2]), inc=(sm["bload"], 16))
    b.do(POOL, "dma_start", out=B2_sb[:, :], in_=B2in[:, :], inc=(sm["bload"], 16))
    t_bload = sm["bload"].n

    w1_tok = [0, 0, 0, 0]
    w1_free = [0, 0, 0, 0]
    w2_tok = [0, 0]
    w2_free = [0, 0]
    m1ps_free = [0, 0]
    m2ps_free = [0, 0]
    gelu_tok = {}
    x6_free = [0, 0]

    def load_w1(e, fc, par):
        b.wait(POOL, sm["act"], w1_free[par])
        b.do(POOL, "dma_start",
             out=w1_sb[par][:, :].rearrange("p (dc m) -> p dc m", m=128),
             in_=w1_ch[e, fc].rearrange("(dc p) m -> p dc m", p=128),
             inc=(ss_w1[par], 16))
        w1_tok[par] = ss_w1[par].n

    def load_w2(e, g, par):
        b.wait(POOL, sm["pe"], w2_free[par])
        rows = (FC // G2) * 128
        b.do(POOL, "dma_start",
             out=w2g_sb[par][:, :].rearrange("p (fc n) -> p fc n", n=D),
             in_=w2T[e, g * rows:(g + 1) * rows, :].rearrange("(fc p) n -> p fc n", p=128),
             inc=(ss_w2[par], 16))
        w2_tok[par] = ss_w2[par].n

    b.wait(PE, sm["dve"], t_comb)
    b.wait(PE, sm["bload"], t_bload)
    b.wait(ACT, sm["bload"], t_bload)

    fcg = 0
    w2i = 0

    def emit_b2_init(half):
        nonlocal_m2 = None
        for st in range(8):
            sc = half * 8 + st
            parp = st % 2
            b.wait(PE, sm["dve"], m2ps_free[parp])
            b.do(PE, "matmul", psM2[parp][:, 0:512], combT_sb[0:9, sc * 128:(sc + 1) * 128],
                 B2_sb[0:9, 0:512], start=True, stop=True)
            b.do(PE, "matmul", psM2[parp][:, 512:1024], combT_sb[0:9, sc * 128:(sc + 1) * 128],
                 B2_sb[0:9, 512:1024], start=True, stop=True, inc=(sm["pe"], 1))
            b.wait(DVE, sm["pe"], sm["pe"].n)
            b.do(DVE, "tensor_copy", acc_sb[:, st * D:(st + 1) * D], psM2[parp][:, :],
                 inc=(sm["dve"], 1), plain=True)
            m2ps_free[parp] = sm["dve"].n

    for half in range(2):
        for e in range(NE):
            for fc in range(FC):
                par = fcg % 4
                fcg += 1
                parp1 = fc % 2
                load_w1(e, fc, par)
                b.wait(PE, ss_w1[par], w1_tok[par])
                b.wait(PE, sm["act"], m1ps_free[parp1])
                for dcd in range(DC):
                    for nn in range(2):
                        b.do(PE, "matmul",
                             psM1[parp1][:, nn * 512:(nn + 1) * 512],
                             w1_sb[par][:, dcd * 128:(dcd + 1) * 128],
                             h2T_view[:, dcd, half * 1024 + nn * 512: half * 1024 + (nn + 1) * 512],
                             start=(dcd == 0), stop=(dcd == DC - 1),
                             inc=(sm["pe"], 1) if (dcd == DC - 1 and nn == 1) else None)
                t_m1 = sm["pe"].n
                b.wait(ACT, sm["pe"], t_m1)
                lastfc = (fc == FC - 1)
                b.do(ACT, "activation", out=h1T_sb[:, fc * 1024:(fc + 1) * 1024],
                     in_=psM1[parp1][:, :], func=AF.Gelu_apprx_tanh,
                     bias=b1_sb[:, e * FC + fc: e * FC + fc + 1], inc=(sm["act"], 1),
                     plain=not lastfc)
                gelu_tok[fc] = sm["act"].n + (0 if lastfc else 1)
                m1ps_free[parp1] = sm["act"].n
                w1_free[par] = sm["act"].n
            if e == 0:
                emit_b2_init(half)
            load_w2(e, 0, w2i % 2)
            for g in range(G2):
                parw = w2i % 2
                w2i += 1
                if g + 1 < G2:
                    load_w2(e, g + 1, w2i % 2)
                b.wait(PE, ss_w2[parw], w2_tok[parw])
                npg = FC // G2
                for st in range(8):
                    sc = half * 8 + st
                    parp = st % 2
                    b.wait(PE, sm["dve"], m2ps_free[parp])
                    for i in range(npg):
                        fc = g * npg + i
                        if st == 0:
                            b.wait(PE, sm["act"], gelu_tok[fc])
                        for nn in range(2):
                            b.do(PE, "matmul",
                                 psM2[parp][:, nn * 512:(nn + 1) * 512],
                                 h1T_sb[:, fc * 1024 + st * 128: fc * 1024 + (st + 1) * 128],
                                 w2g_sb[parw][:, i * 1024 + nn * 512: i * 1024 + (nn + 1) * 512],
                                 start=(i == 0), stop=(i == npg - 1),
                                 inc=(sm["pe"], 1) if (i == npg - 1 and nn == 1) else None)
                    t_m2 = sm["pe"].n
                    b.wait(DVE, sm["pe"], t_m2)
                    b.do(DVE, "scalar_tensor_tensor", acc_sb[:, st * D:(st + 1) * D],
                         psM2[parp][:, :], comb_sb[:, sc * NE + e: sc * NE + e + 1],
                         acc_sb[:, st * D:(st + 1) * D], ALU.mult, ALU.add, inc=(sm["dve"], 1),
                         plain=True)
                    m2ps_free[parp] = sm["dve"].n
                w2_free[parw] = sm["pe"].n
        for st in range(8):
            sc = half * 8 + st
            par = st % 2
            b.wait(SYNC, ss_oo[par], x6_free[par])
            b.wait(SYNC, ss_x1o[0], t_x1_all[0])
            b.wait(SYNC, ss_x1o[1], t_x1_all[1])
            b.do(SYNC, "dma_start", out=x6b[par][:, :], in_=x1d[sc * 128:(sc + 1) * 128, :],
                 inc=(ss_x6[par], 16))
            t_xl = ss_x6[par].n
            b.do(DVE, "tensor_mul", scr6[:, :], acc_sb[:, st * D:(st + 1) * D], gb_sb[:, :])
            b.wait(DVE, ss_x6[par], t_xl)
            b.do(DVE, "tensor_add", x6b[par][:, :], x6b[par][:, :], scr6[:, :], inc=(sm["dve"], 1))
            t_o = sm["dve"].n
            b.wait(SYNC, sm["dve"], t_o)
            b.do(SYNC, "dma_start", out=out[sc * 128:(sc + 1) * 128, :], in_=x6b[par][:, :],
                 inc=(ss_oo[par], 16))
            x6_free[par] = ss_oo[par].n

    return _finish(b, sm)


def _finish(b, sm):
    for s in b.final_waits:
        b.wait(SYNC, s, s.n)
    if sm["outst"].n:
        b.wait(SYNC, sm["outst"], sm["outst"].n)
    if sm["dbgst"].n:
        eng = POOL if b.ops[POOL] else SYNC
        b.wait(eng, sm["dbgst"], sm["dbgst"].n)
    b.flush()
    b.stack.close()
    for scn in reversed(list(b.scopes)):
        b.scopes.pop(scn).close()
    return b.nc


def _rel_buckets_1d():
    NUM_BUCKETS, MAX_DISTANCE = 32, 128
    half = NUM_BUCKETS // 2
    max_exact = half // 2
    coeff = (half - max_exact) / math.log(MAX_DISTANCE / max_exact)
    d = np.arange(-2047, 2048)
    arel = np.abs(d)
    base = np.where(d > 0, half, 0)
    distant = max_exact + (np.log(np.maximum(arel, 1).astype(np.float32) / max_exact) * coeff).astype(np.int32)
    distant = np.minimum(distant, half - 1)
    return base + np.where(arel < max_exact, arel, distant)


def host_layout(inputs):
    qkv_wT = np.ascontiguousarray(inputs["L0_attn_qkv_w"].T)
    qkvT_ch = np.ascontiguousarray(qkv_wT.reshape(D, 24, 128).transpose(1, 0, 2))
    owT = np.ascontiguousarray(inputs["L0_attn_o_w"].T)
    wmod_ch = np.ascontiguousarray(inputs["L0_wmod"].reshape(D, 12, 512).transpose(1, 0, 2))
    rwT = np.ascontiguousarray(inputs["L0_ffn_router_w"].T)
    rwT_ch = np.ascontiguousarray(rwT.reshape(8, 128, 8))
    rbv = np.ascontiguousarray(inputs["L0_ffn_router_b"].reshape(8, 1))
    w1 = np.concatenate([inputs["L0_ffn_spec_w1"], inputs["L0_ffn_shared_w1"][None]], axis=0)
    w1T = w1.transpose(0, 2, 1)
    w1_ch = np.ascontiguousarray(w1T.reshape(NE, D, FC, 128).transpose(0, 2, 1, 3))
    w2 = np.concatenate([inputs["L0_ffn_spec_w2"], inputs["L0_ffn_shared_w2"][None]], axis=0)
    w2Tv = np.ascontiguousarray(w2.transpose(0, 2, 1))
    b1 = np.concatenate([inputs["L0_ffn_spec_b1"], inputs["L0_ffn_shared_b1"][None]], axis=0)
    b1tv = np.ascontiguousarray(b1.reshape(NE, FC, 128).transpose(0, 2, 1))
    B2 = np.ascontiguousarray(
        np.concatenate([inputs["L0_ffn_spec_b2"], inputs["L0_ffn_shared_b2"][None]], axis=0))
    buckets = _rel_buckets_1d()
    tab = inputs["L0_attn_emb_weight"][buckets, :]
    tabr = tab[::-1]
    Tpv = np.zeros((NH, 128, EB), np.float32)
    for p in range(128):
        Tpv[:, p, p + 1:] = tabr[0:EB - 1 - p, :].T
    return dict(qkvT_ch=qkvT_ch, owT=owT, wmod_ch=wmod_ch, rwT_ch=rwT_ch, rb=rbv,
                w1_ch=w1_ch, w2T=w2Tv, b1t=b1tv, B2in=B2, Tp=Tpv,
                identf=np.eye(128, dtype=np.float32))


def core_inputs(inputs, shared, bidx):
    d = dict(shared)
    d["x"] = np.ascontiguousarray(inputs["L0_attn_x"][bidx])
    d["cvec"] = np.ascontiguousarray(inputs["L0_c"][bidx].reshape(1, D))
    return d


def kernel(**inputs):
    inputs = {k: np.asarray(v) for k, v in inputs.items()}
    shared = host_layout(inputs)
    nc = build_program("full")
    in_maps = [core_inputs(inputs, shared, bi) for bi in range(8)]
    res = run_bass_kernel_spmd(nc, in_maps, core_ids=list(range(8)))
    return np.stack([np.asarray(r["out"]) for r in res.results], axis=0)
